# version 11
# speedup vs baseline: 1.3211x; 1.0305x over previous
import contextlib
import numpy as np
import ml_dtypes

import concourse.bass as bass
import concourse.mybir as mybir
from concourse.bass_utils import run_bass_kernel_spmd
from concourse.bass_core_rust import IndirectOffsetOnAxis

F32 = mybir.dt.float32
BF16 = mybir.dt.bfloat16
I32 = mybir.dt.int32
U32 = mybir.dt.uint32
AF = mybir.ActivationFunctionType
ALU = mybir.AluOpType
AX = mybir.AxisListType

D = 1024
S = 4096
T = 2048
R = 128
EPS = 1e-5
HID = 3072
CFF = float(np.float32(1.0 / 127.0))
ROPE_BASE = 100000.0
BF = ml_dtypes.bfloat16

PERM = []
for _t in range(4):
    for _h in range(4 * _t, 4 * _t + 4):
        PERM.extend(range(64 * _h, 64 * _h + 32))
    for _h in range(4 * _t, 4 * _t + 4):
        PERM.extend(range(64 * _h + 32, 64 * _h + 64))
PERM = np.array(PERM)

RX_SX, RX_P1, RX_P2 = 1, 2, 3
RX_SXX = 4
RX_M, RX_RS, RX_TMP, RX_TMP2 = 5, 6, 7, 8
RX_STOK = 9
RX_SXLO, RX_P3 = 10, 11
RX_MQ, RX_SQ1, RX_SQ2 = 12, 13, 14
RX_BLK, RX_IOTA64, RX_SCAL = 15, 16, 18
RX_U, RX_T8V, RX_BLK4 = 20, 21, 22

A_LAT = 0
A_XHI = 4096
A_XLO = 8192
A_X2B = 10240
A_COSS = 4096
A_SINS = 6144
A_KSEL = 8192
A_KROT = 10240
A_KFM = 12288
A_VBUF = 14336
A_MASK = 16512
A_WKVD = 12288
A_WKU = 21376
A_LSEL = 20608
A_LSELF = 20864
A_KTMP = 21120
A_WQ = 0
A_XQH = 8192
A_XSQ = 36992
A_QBUF = 20608
A_TABC = 36992
A_TABSP = 39040
A_TABSN = 41088
A_FMTAB = 43136
A_QT1, A_QT2, A_QT3, A_QT4 = 43136, 45184, 47232, 8192
A_PBUF = 0
A_PTMP = 4096
A_WOUT = 16512
A_OBUF = 36992
A_X2L = 0
A_GU = 8192
A_WD = 12288
A_SIL = 15360
A_UP = 16384
A_HIDN = 17408
A_DN = 41984
ABF_SZ = 50176


def atm_off(qtile):
    return 6144 + qtile * 1024 if qtile < 6 else 36992 + (qtile - 6) * 1024


def afm_off(dt):
    return 24704 + dt * 2048 if dt < 6 else 12288 + (dt - 6) * 2048


def obuf_off(mt):
    return 36992 + mt * 2048 if mt < 6 else (mt - 6) * 2048


def build_nc():
    nc = bass.Bass("TRN2", target_bir_lowering=False)

    xT_d = nc.dram_tensor("xT", [D, S], F32, kind="ExternalInput")
    xTq_d = nc.dram_tensor("xTq", [D, T], F32, kind="ExternalInput")
    xtm_d = nc.dram_tensor("xtm", [S, D], F32, kind="ExternalInput")
    qiota_d = nc.dram_tensor("qiota", [1, T], F32, kind="ExternalInput")
    wqT_d = nc.dram_tensor("wqT", [D, D], F32, kind="ExternalInput")
    wkvdT_d = nc.dram_tensor("wkvdT", [D, R], F32, kind="ExternalInput")
    wkuT_d = nc.dram_tensor("wkuT", [R, 2 * D], F32, kind="ExternalInput")
    woutT_d = nc.dram_tensor("woutT", [D, D], F32, kind="ExternalInput")
    wscT_d = nc.dram_tensor("wscT", [128, 8], F32, kind="ExternalInput")
    wgT_d = nc.dram_tensor("wgT", [D, HID], I32, kind="ExternalInput")
    wuT_d = nc.dram_tensor("wuT", [D, HID], I32, kind="ExternalInput")
    wdT_d = nc.dram_tensor("wdT", [HID, D], I32, kind="ExternalInput")
    cosfm_d = nc.dram_tensor("cosfm", [128, T], BF16, kind="ExternalInput")
    sinpfm_d = nc.dram_tensor("sinpfm", [128, T], BF16, kind="ExternalInput")
    sinnfm_d = nc.dram_tensor("sinnfm", [128, T], BF16, kind="ExternalInput")
    costm_d = nc.dram_tensor("costm", [S, D], BF16, kind="ExternalInput")
    sintm_d = nc.dram_tensor("sintm", [S, D], BF16, kind="ExternalInput")
    y_d = nc.dram_tensor("y", [D, T], F32, kind="ExternalOutput")

    lat_dram = nc.dram_tensor("lat_dram", [S, R], BF16, kind="Internal")
    wgu_d = nc.dram_tensor("wgu", [24, 128, 2048], BF16, kind="Internal")
    wdb_d = nc.dram_tensor("wdb", [8, 128, 3072], BF16, kind="Internal")
    wob_d = nc.dram_tensor("wob", [128, 8192], BF16, kind="Internal")

    ctx = contextlib.ExitStack()
    SB = lambda n, s, d: ctx.enter_context(nc.sbuf_tensor(n, s, d))

    rows = SB("rows", [128, S], F32)
    xq = SB("xq", [128, 8 * T], F32)
    rows_bf = SB("rows_bf", [128, 4640], BF16)
    AFX = SB("AFX", [128, 2048], F32)
    ABF = SB("ABF", [128, ABF_SZ], BF16)
    ident = SB("ident", [128, 128], BF16)
    identf = SB("identf", [128, 128], F32)
    onec_bf = SB("onec_bf", [128, 1], BF16)
    onec_f = SB("onec_f", [128, 1], F32)
    onesrow_f = SB("onesrow_f", [128, 128], F32)
    one1_f = SB("one1_f", [1, 1], F32)
    PAN = SB("PAN", [128, 8 * 32], F32)
    kvx = SB("kvx", [128, 8 * 131], BF16)
    statB = SB("statB", [128, 8 * 2], BF16)
    halfon = SB("halfon", [128, 2], F32)
    csc_col = SB("csc_col", [128, 1], F32)
    blk2 = SB("blk2", [2, 32], F32)
    wsc_f = SB("wsc_f", [128, 8], F32)
    idx8 = SB("idx8", [1, 8], U32)
    idx_u32 = SB("idx_u32", [128, 2], U32)
    pos_f = SB("pos_f", [128, 2], F32)
    msel_c = SB("msel_c", [128, 2], F32)
    rsel_c = SB("rsel_c", [128, 2], F32)
    rcols = SB("rcols", [128, 32], BF16)
    mlat_c = SB("mlat_c", [128, 1], BF16)
    w1d_c = SB("w1d_c", [128, 1], BF16)
    recb = SB("recb", [128, 8], F32)
    constc = SB("constc", [128, 928], F32)
    _CONSTS = [EPS, 0.125, -0.5, CFF, CFF * CFF, 1.0, -1.0, 64.0]
    for _i, _cv in enumerate(_CONSTS):
        nc.const_aps.register(F32, _cv, constc[:, _i:_i + 1])

    def ab(off, w, p0=0, p1=128):
        return ABF[p0:p1, off:off + w]

    class SemC:
        def __init__(self, name):
            self.h = ctx.enter_context(nc.semaphore(name))
            self.n = 0

        def inc(self, inst, k=1):
            self.n += k
            return inst.then_inc(self.h, k)

    s_dx = SemC("s_dx")
    s_dw = SemC("s_dw")
    s_dh = SemC("s_dh")
    s_v = SemC("s_v")
    s_s = SemC("s_s")
    s_t = SemC("s_t")
    s_out = SemC("s_out")
    s_gat = SemC("s_gat")
    s_ff = SemC("s_ff")
    s_y = SemC("s_y")
    s_pc = SemC("s_pc")
    s_pw = SemC("s_pw")
    s_wa = SemC("s_wa")
    s_wb = SemC("s_wb")
    s_g = SemC("s_g")

    _pc_jobs = []
    for _ht in range(24):
        _pc_jobs.append((
            wgu_d.ap()[_ht][:, 0:1024].rearrange("p (dt c) -> p dt c", c=128),
            wgT_d.ap()[:, _ht * 128:(_ht + 1) * 128].rearrange(
                "(dt p) c -> p dt c", p=128)))
        _pc_jobs.append((
            wgu_d.ap()[_ht][:, 1024:2048].rearrange("p (dt c) -> p dt c",
                                                    c=128),
            wuT_d.ap()[:, _ht * 128:(_ht + 1) * 128].rearrange(
                "(dt p) c -> p dt c", p=128)))
    for _mt in range(8):
        _pc_jobs.append((
            wdb_d.ap()[_mt].rearrange("p (kt c) -> p kt c", c=128),
            wdT_d.ap()[:, _mt * 128:(_mt + 1) * 128].rearrange(
                "(kt p) c -> p kt c", p=128)))
    PC_TOTAL = 16 * len(_pc_jobs)
    _pc_state = [0]

    def issue_precast(g, njobs):
        while njobs > 0 and _pc_state[0] < len(_pc_jobs):
            i = _pc_state[0]
            dst, src = _pc_jobs[i]
            s_pc.inc(g.dma_start(dst, src), 16)
            _pc_state[0] = i + 1
            njobs -= 1

    with nc.Block() as b:
        @b.gpsimd
        def _(g):
            for _i, _cv in enumerate(_CONSTS):
                g.memset(constc[:, _i:_i + 1], _cv)
            g.memset(onec_bf[:, :], 1.0)
            g.memset(onec_f[:, :], 1.0)
            g.memset(onesrow_f[:, :], 1.0)
            g.memset(one1_f[:, :], 1.0)
            g.memset(ident[:, :], 1.0)
            g.memset(identf[:, :], 1.0)
            g.iota(constc[0:1, 8:8 + 64], [[1, 64]],
                   channel_multiplier=0, allow_small_or_imprecise_dtypes=True)
            g.memset(ab(A_VBUF, 2176), 1.0)
            s_ff.inc(g.dma_start(
                kvx[:, :].rearrange("p (dt c) -> p dt c", c=131)[:, :, 0:128],
                wkvdT_d.ap().rearrange("(dt p) r -> p dt r", p=128)), 16)
            s_ff.inc(g.dma_start(ab(A_WKU, 2048), wkuT_d.ap()), 16)

        @b.sync
        def _(sy):
            s_dw.inc(sy.dma_start(wsc_f[:, :], wscT_d.ap()), 16)
            s_dw.inc(sy.dma_start(ab(A_FMTAB, 2048), cosfm_d.ap()), 16)
            s_dw.inc(sy.dma_start(ab(A_FMTAB + 2048, 2048), sinpfm_d.ap()), 16)
            s_dw.inc(sy.dma_start(ab(A_FMTAB + 4096, 2048), sinnfm_d.ap()), 16)
            s_dw.inc(sy.dma_start(
                xq[:, :].rearrange("p (dt t) -> p dt t", t=T),
                xTq_d.ap().rearrange("(dt p) t -> p dt t", p=128)), 16)

    with nc.Block() as b:
        @b.gpsimd
        def _(g):
            g.affine_select(ident[:, :], ident[:, :], [[1, 128]], ALU.is_equal,
                            0.0, channel_multiplier=-1)
            g.affine_select(identf[:, :], identf[:, :], [[1, 128]],
                            ALU.is_equal, 0.0, channel_multiplier=-1)

        @b.vector
        def _(v):
            v.wait_ge(s_ff.h, s_ff.n)
            v.wait_ge(s_dw.h, s_dw.n)
            v.memset(halfon[:, :], 0.0)
            v.drain()
            v.memset(halfon[0:64, 0:1], 1.0)
            v.memset(halfon[64:128, 1:2], 1.0)
            for d in range(8):
                v.memset(kvx[:, 131 * d + 128:131 * d + 129], 1.0)
                v.memset(statB[:, 2 * d:2 * d + 1], 1.0)
                v.tensor_copy(kvx[:, 131 * d + 129:131 * d + 130],
                              wsc_f[:, d:d + 1])
                v.tensor_copy(statB[:, 2 * d + 1:2 * d + 2], wsc_f[:, d:d + 1])
            v.drain()
            for d in range(8):
                v.tensor_tensor(kvx[:, 131 * d + 130:131 * d + 131],
                                wsc_f[:, d:d + 1],
                                kvx[:, 131 * d + 129:131 * d + 130],
                                ALU.subtract)

    NCH, CW = 32, 128
    with (
        nc.psum_tensor("psL0", [128, 131], F32) as psL0,
        nc.psum_tensor("psL1", [128, 131], F32) as psL1,
        nc.psum_tensor("psP0", [128, 3], F32) as psP0,
        nc.psum_tensor("psP1", [128, 3], F32) as psP1,
        nc.Block() as b,
    ):
        psL = [psL0, psL1]
        psS = [psP0, psP1]

        def xck(c, d=None):
            off = (c % 2) * 1024
            if d is None:
                return AFX[:, off:off + 1024]
            return AFX[:, off + d * CW:off + (d + 1) * CW]

        def xhck(c, d=None):
            off = A_XHI + (c % 2) * 1024
            if d is None:
                return ab(off, 1024)
            return ab(off + d * CW, CW)

        def xlck(c, d=None):
            off = A_XLO + (c % 2) * 1024
            if d is None:
                return ab(off, 1024)
            return ab(off + d * CW, CW)

        def x2ck(c, d=None):
            off = A_X2B + (c % 2) * 1024
            if d is None:
                return ab(off, 1024)
            return ab(off + d * CW, CW)

        def xsrc(c):
            return xT_d.ap()[:, c * CW:(c + 1) * CW].rearrange(
                "(dt p) t -> p dt t", p=128)

        def xwait(eng, c):
            if c % 2 == 0:
                eng.wait_ge(s_dx.h, 16 * (c // 2 + 1))
            else:
                eng.wait_ge(s_dh.h, 16 * (c // 2 + 1))

        @b.sync
        def _(sy):
            for c in range(0, NCH, 2):
                if c >= 2:
                    sy.wait_ge(s_v.h, 2 * (c - 2) + 1)
                    sy.wait_ge(s_s.h, 2 * (c - 2) + 2)
                s_dx.inc(sy.dma_start(
                    xck(c).rearrange("p (dt t) -> p dt t", t=CW), xsrc(c)), 16)

        @b.gpsimd
        def _(g):
            for c in range(NCH):
                if c % 2 == 1:
                    if c >= 2:
                        g.wait_ge(s_v.h, 2 * (c - 2) + 1)
                        g.wait_ge(s_s.h, 2 * (c - 2) + 2)
                    s_dh.inc(g.dma_start(
                        xck(c).rearrange("p (dt t) -> p dt t", t=CW),
                        xsrc(c)), 16)
                if c >= 2:
                    g.wait_ge(s_v.h, 2 * (c - 2) + 2)
                    s_out.inc(g.dma_start(
                        lat_dram.ap()[(c - 2) * CW:(c - 1) * CW, :],
                        ab(A_LAT + (c - 2) * R, R)), 16)
            for c in (NCH - 2, NCH - 1):
                g.wait_ge(s_v.h, 2 * c + 2)
                s_out.inc(g.dma_start(
                    lat_dram.ap()[c * CW:(c + 1) * CW, :],
                    ab(A_LAT + c * R, R)), 16)

        @b.scalar
        def _(s):
            for c in range(NCH):
                xwait(s, c)
                if c >= 2:
                    s.wait_ge(s_t.h, 2 * (c - 2) + 2)
                    s.wait_ge(s_v.h, 2 * (c - 2) + 1)
                s_s.inc(s.activation(xhck(c), xck(c), AF.Copy))
                s_s.inc(s.activation(x2ck(c), xck(c), AF.Square))

        @b.vector
        def _(v):
            for c in range(NCH):
                xwait(v, c)
                v.wait_ge(s_s.h, 2 * c + 1)
                if c >= 2:
                    v.wait_ge(s_t.h, 2 * (c - 2) + 2)
                s_v.inc(v.tensor_tensor(xlck(c), xck(c), xhck(c),
                                        ALU.subtract))
                v.wait_ge(s_t.h, 2 * c + 2)
                v.tensor_copy(ab(A_LAT + c * R, R), psL[c % 2][:, 0:128])
                v.tensor_copy(
                    PAN[:, :].rearrange("p (q w) -> p q w", w=32)[:, 0:3,
                                                                  c:c + 1],
                    psL[c % 2][:, 128:131].rearrange("p (a q) -> p q a", a=1))
                v.tensor_copy(
                    PAN[:, :].rearrange("p (q w) -> p q w", w=32)[:, 3:6,
                                                                  c:c + 1],
                    psS[c % 2][:, 0:3].rearrange("p (a q) -> p q a", a=1))
                v.drain()
                s_v.inc(v.engine_nop())

        @b.tensor
        def _(t):
            for c in range(NCH):
                t.wait_ge(s_s.h, 2 * c + 1)
                if c >= 2:
                    t.wait_ge(s_v.h, 2 * (c - 2) + 2)
                for d in range(8):
                    mm = t.matmul(psL[c % 2][:, 0:131], xhck(c, d),
                                  kvx[:, 131 * d:131 * (d + 1)],
                                  start=(d == 0), stop=(d == 7))
                s_t.inc(mm)
                t.wait_ge(s_v.h, 2 * c + 1)
                t.wait_ge(s_s.h, 2 * c + 2)
                for d in range(8):
                    t.matmul(psS[c % 2][:, 0:2], xlck(c, d),
                             statB[:, 2 * d:2 * d + 2],
                             start=(d == 0), stop=False)
                for d in range(8):
                    mm = t.matmul(psS[c % 2][:, 2:3], x2ck(c, d),
                                  onec_bf[:, :],
                                  start=False, stop=(d == 7))
                s_t.inc(mm)

    C_IOTA, C_OFF, C_CSC, C_MINP, C_SMR, C_BLK, C_BLK4, C_T8V = \
        8, 72, 328, 329, 330, 331, 395, 399
    PANv = PAN[:, :].rearrange("p (q w) -> p q w", w=32)

    def pq(q):
        return PAN[:, 32 * q:32 * (q + 1)]

    with (
        nc.psum_tensor("psRowA", [1, 512], F32) as psRowA,
        nc.psum_tensor("psColA", [128, 1], F32) as psColA,
        nc.psum_tensor("psColB", [128, 1], F32) as psColB,
        nc.psum_tensor("psBLK", [2, 32], F32) as psBLK,
        nc.psum_tensor("psSMR", [1, 32], F32) as psSMR,
        nc.Block() as b,
    ):
        @b.tensor
        def _(t):
            for d in range(8):
                t.matmul(psRowA[:, 0:1], wsc_f[:, d:d + 1], onec_f[:, :],
                         start=(d == 0), stop=(d == 7))
            for d in range(8):
                t.matmul(psRowA[:, 128:128 + R], onec_bf[:, :],
                         kvx[:, 131 * d:131 * d + 128],
                         start=(d == 0), stop=(d == 7), skip_group_check=True)
            for d in range(8):
                t.matmul(psColA[:, :], kvx[:, 131 * d:131 * d + 128],
                         onec_bf[:, :], start=(d == 0), stop=(d == 7))
            t.drain()
            s_t.inc(t.nop())

        @b.vector
        def _(v):
            v.wait_ge(s_t.h, s_t.n)
            v.tensor_copy(constc[0:1, C_CSC:C_CSC + 1], psRowA[:, 0:1])
            v.tensor_copy(rows_bf[0:1, 256:256 + R], psRowA[:, 128:128 + R])
            v.tensor_copy(w1d_c[:, :], psColA[:, :])
            v.tensor_tensor(pq(6), pq(0), pq(3), ALU.add)
            v.drain()
            v.tensor_scalar_mul(pq(6), pq(6), 1.0 / 1024.0)
            v.tensor_scalar_mul(pq(5), pq(5), 1.0 / 1024.0)
            v.drain()
            v.tensor_tensor(pq(0), pq(6), pq(6), ALU.mult)
            v.drain()
            v.tensor_tensor(pq(5), pq(5), pq(0), ALU.subtract)
            v.drain()
            s_v.inc(v.engine_nop())

        @b.scalar
        def _(s):
            s.wait_ge(s_v.h, s_v.n)
            s.activation(pq(7), pq(5), AF.Ln, bias=EPS)
            s.drain()
            s.activation(pq(7), pq(7), AF.Exp, scale=-0.5)
            s.drain()
            s_s.inc(s.nop())

        @b.tensor
        def _(t):
            t.wait_ge(s_v.h, s_v.n)
            t.matmul(psColB[:, :], onesrow_f[0:1, :],
                     constc[0:1, C_CSC:C_CSC + 1], start=True, stop=True)
            t.drain()
            s_t.inc(t.nop())

        @b.vector
        def _(v):
            v.wait_ge(s_t.h, s_t.n)
            v.wait_ge(s_s.h, s_s.n)
            v.tensor_copy(csc_col[:, :], psColB[:, :])
            v.tensor_copy(rcols[:, :], pq(7))
            v.tensor_tensor(pq(1), pq(1), pq(2), ALU.add)
            v.drain()
            v.tensor_tensor(pq(1), pq(1), pq(4), ALU.add)
            v.tensor_tensor(pq(2), pq(6), pq(7), ALU.mult)
            v.drain()
            v.tensor_scalar(pq(0), pq(6), csc_col[:, 0:1], None, op0=ALU.mult)
            v.drain()
            v.tensor_tensor(pq(1), pq(1), pq(0), ALU.subtract)
            v.drain()
            v.tensor_tensor(pq(1), pq(1), pq(7), ALU.mult)
            v.drain()
            s_v.inc(v.engine_nop())

        @b.tensor
        def _(t):
            t.wait_ge(s_v.h, s_v.n)
            t.matmul(psBLK[:, :], halfon[:, :], pq(1), start=True, stop=True)
            t.matmul(psSMR[:, :], onec_f[:, :], pq(2), start=True, stop=True)
            t.drain()
            s_t.inc(t.nop())

        @b.vector
        def _(v):
            v.wait_ge(s_t.h, s_t.n)
            v.tensor_copy(blk2[:, :], psBLK[:, :])
            v.tensor_scalar(constc[0:1, 416:448], psSMR[:, :], -1.0, None,
                            op0=ALU.mult, op1=ALU.add,
                            accum_out=constc[0:1, C_SMR:C_SMR + 1])
            v.drain()
            v.tensor_copy(rows_bf[0:1, 384:385], constc[0:1, C_SMR:C_SMR + 1])
            v.drain()
            s_v.inc(v.engine_nop())

        @b.gpsimd
        def _(g):
            g.wait_ge(s_v.h, s_v.n)
            for h2 in range(2):
                s_gat.inc(g.dma_start(
                    constc[0:1, C_BLK + 32 * h2:C_BLK + 32 * h2 + 32],
                    blk2[h2:h2 + 1, :]), 16)

    with nc.Block() as b:
        @b.vector
        def _(v):
            v.wait_ge(s_gat.h, s_gat.n)
            v.max(constc[0:1, C_T8V:C_T8V + 8], constc[0:1, C_BLK:C_BLK + 64])
            v.drain()
            v.max_index(idx8[:, :], constc[0:1, C_T8V:C_T8V + 8],
                        constc[0:1, C_BLK:C_BLK + 64])
            v.drain()
            v.tensor_copy(constc[0:1, C_BLK4:C_BLK4 + 4], idx8[0:1, 0:4])
            v.drain()
            v.tensor_scalar(constc[0:1, C_T8V:C_T8V + 4],
                            constc[0:1, C_BLK4:C_BLK4 + 4], 32.0, 4032.0,
                            op0=ALU.is_ge, op1=ALU.mult)
            v.drain()
            v.tensor_scalar_mul(constc[0:1, C_BLK4:C_BLK4 + 4],
                                constc[0:1, C_BLK4:C_BLK4 + 4], 128.0)
            v.drain()
            v.tensor_tensor(constc[0:1, C_BLK4:C_BLK4 + 4],
                            constc[0:1, C_BLK4:C_BLK4 + 4],
                            constc[0:1, C_T8V:C_T8V + 4], ALU.subtract)
            v.drain()
            v.tensor_reduce(constc[0:1, C_MINP:C_MINP + 1],
                            constc[0:1, C_BLK4:C_BLK4 + 4], AX.X, ALU.min)
            v.drain()
            for j in range(4):
                v.tensor_scalar(constc[0:1, C_OFF + 64 * j:C_OFF + 64 * j + 64],
                                constc[0:1, C_IOTA:C_IOTA + 64],
                                constc[0:1, C_BLK4 + j:C_BLK4 + j + 1], None,
                                op0=ALU.add)
                v.drain()
                s_v.inc(v.engine_nop())

    with (
        nc.psum_tensor("psCol2", [128, 8], F32) as psCol2,
        nc.Block() as b,
    ):
        @b.tensor
        def _(t):
            for cc in range(2):
                t.matmul(psCol2[:, cc:cc + 1],
                         constc[0:1, C_OFF + 128 * cc:C_OFF + 128 * cc + 128],
                         one1_f[:, :], start=(cc == 0), stop=True,
                         skip_group_check=True)
            t.drain()
            s_t.inc(t.nop())

        @b.vector
        def _(v):
            v.wait_ge(s_t.h, s_t.n)
            v.tensor_copy(pos_f[:, :], psCol2[:, 0:2])
            v.tensor_copy(idx_u32[:, :], psCol2[:, 0:2])

        @b.sync
        def _(sy):
            s_dw.inc(sy.dma_start(rows[0:1, 0:T], qiota_d.ap()), 16)

    with nc.Block() as b:
        @b.gpsimd
        def _(g):
            g.wait_ge(s_out.h, s_out.n)
            for cc in range(2):
                s_gat.inc(g.indirect_dma_start(
                    ab(A_LSEL + cc * R, R), None, lat_dram.ap(),
                    IndirectOffsetOnAxis(idx_u32[:, cc:cc + 1], 0)), 16)
                s_gat.inc(g.indirect_dma_start(
                    AFX[:, cc * 1024:(cc + 1) * 1024], None, xtm_d.ap(),
                    IndirectOffsetOnAxis(idx_u32[:, cc:cc + 1], 0)), 16)
                s_gat.inc(g.indirect_dma_start(
                    ab(A_COSS + cc * D, D), None, costm_d.ap(),
                    IndirectOffsetOnAxis(idx_u32[:, cc:cc + 1], 0)), 16)
                s_gat.inc(g.indirect_dma_start(
                    ab(A_SINS + cc * D, D), None, sintm_d.ap(),
                    IndirectOffsetOnAxis(idx_u32[:, cc:cc + 1], 0)), 16)

    with nc.Block() as b:
        @b.vector
        def _(v):
            v.wait_ge(s_gat.h, s_gat.n)
            for cc in range(2):
                xs = AFX[:, cc * 1024:(cc + 1) * 1024]
                v.reduce_sum(msel_c[:, cc:cc + 1], xs, axis=AX.X)
                v.drain()
                v.tensor_tensor(xs, xs, xs, ALU.mult)
                v.drain()
                v.reduce_sum(rsel_c[:, cc:cc + 1], xs, axis=AX.X)
            v.drain()
            v.tensor_scalar_mul(msel_c[:, :], msel_c[:, :], 1.0 / 1024.0)
            v.tensor_scalar_mul(rsel_c[:, :], rsel_c[:, :], 1.0 / 1024.0)
            v.drain()
            v.tensor_tensor(AFX[:, 0:2], msel_c[:, :], msel_c[:, :], ALU.mult)
            v.drain()
            v.tensor_tensor(rsel_c[:, :], rsel_c[:, :], AFX[:, 0:2],
                            ALU.subtract)
            v.drain()
            s_v.inc(v.engine_nop())

        @b.scalar
        def _(s):
            s.wait_ge(s_v.h, s_v.n)
            s.activation(rsel_c[:, :], rsel_c[:, :], AF.Ln, bias=EPS)
            s.drain()
            s.activation(rsel_c[:, :], rsel_c[:, :], AF.Exp, scale=-0.5)
            s.drain()
            s_s.inc(s.nop())

    with (
        nc.psum_tensor("psTa", [128, 128], BF16) as psTa,
        nc.psum_tensor("psTb", [128, 128], BF16) as psTb,
        nc.psum_tensor("psTf", [1, 128], F32) as psTf,
        nc.psum_tensor("psTf2", [1, 128], F32) as psTf2,
        nc.psum_tensor("psMR", [128, 1], F32) as psMR,
        nc.Block() as b,
    ):
        @b.gpsimd
        def _(g):
            for hf2 in range(2):
                if hf2 == 1:
                    g.wait_ge(s_pw.h, 16)
                s_pw.inc(g.dma_start(
                    wob_d.ap().rearrange("p (dt o) -> p dt o", o=D)[
                        :, hf2 * 4:(hf2 + 1) * 4, :],
                    woutT_d.ap().rearrange("(dt p) o -> p dt o", p=128)[
                        :, hf2 * 4:(hf2 + 1) * 4, :]), 16)
            issue_precast(g, 20)

        @b.tensor
        def _(t):
            t.transpose(psTa[:, :], ab(A_LSEL, R), ident[:, :])
            t.transpose(psTb[:, :], ab(A_LSEL + R, R), ident[:, :])
            t.transpose(psTf[:, :], msel_c[:, 0:1], identf[:, :])
            t.transpose(psTf2[:, :], msel_c[:, 1:2], identf[:, :])
            for tt in range(32):
                t.matmul(psMR[:, :], ab(A_LAT + tt * R, R), rcols[:, tt:tt + 1],
                         start=(tt == 0), stop=False)
            t.matmul(psMR[:, :], rows_bf[0:1, 256:256 + R],
                     rows_bf[0:1, 384:385],
                     start=False, stop=True)
            t.drain()
            s_t.inc(t.nop())

        @b.vector
        def _(v):
            v.wait_ge(s_t.h, s_t.n)
            v.tensor_copy(ab(A_LSELF, R), psTa[:, :])
            v.tensor_copy(ab(A_LSELF + R, R), psTb[:, :])
            v.tensor_scalar_mul(rows_bf[0:1, 0:128], psTf[0:1, :], -1.0)
            v.tensor_scalar_mul(rows_bf[0:1, 128:256], psTf2[0:1, :], -1.0)
            v.tensor_scalar_mul(mlat_c[:, :], psMR[:, :], 1.0 / S)
            v.drain()
            s_v.inc(v.engine_nop())

    with (
        nc.psum_tensor("psRowB", [1, 1024], F32) as psRowB,
        nc.Block() as b,
    ):
        w1k_t0, w1k_v0 = s_t.n, s_v.n

        @b.tensor
        def _(t):
            for half in range(2):
                if half >= 1:
                    t.wait_ge(s_v.h, w1k_v0 + 1)
                for ch in range(2):
                    t.matmul(psRowB[:, 512 * ch:512 * (ch + 1)], w1d_c[:, :],
                             ab(A_WKU + half * 1024 + 512 * ch, 512),
                             start=True, stop=True, skip_group_check=True)
                t.drain()
                s_t.inc(t.nop())

        @b.vector
        def _(v):
            for half in range(2):
                v.wait_ge(s_t.h, w1k_t0 + half + 1)
                v.tensor_copy(rows_bf[0:1, 512 + half * 1024:
                                      512 + (half + 1) * 1024], psRowB[:, :])
                v.drain()
                s_v.inc(v.engine_nop())

    with (
        nc.psum_tensor("psRowC", [1, 1024], F32) as psRowC,
        nc.Block() as b,
    ):
        @b.tensor
        def _(t):
            for ch in range(2):
                t.matmul(psRowC[:, 512 * ch:512 * (ch + 1)], mlat_c[:, :],
                         ab(A_WKU + D + 512 * ch, 512),
                         start=True, stop=True, skip_group_check=True)
            t.drain()
            s_t.inc(t.nop())

        @b.vector
        def _(v):
            v.wait_ge(s_t.h, s_t.n)
            v.memset(rows_bf[0:1, 2560:3600], 1.0)
            v.drain()
            v.tensor_copy(
                rows_bf[0:1, 2560:3600].rearrange(
                    "p (h e) -> p h e", e=65)[:, :, 0:64],
                psRowC[:, :].rearrange("p (h e) -> p h e", e=64))

    with (
        nc.psum_tensor("psW", [128, 512], F32) as psW,
        nc.psum_tensor("psW2", [128, 512], F32) as psW2,
        nc.Block() as b,
    ):
        psWp = [psW, psW2]

        @b.tensor
        def _(t):
            for mt in range(2):
                for ch in range(4):
                    i = mt * 4 + ch
                    if i >= 2:
                        t.wait_ge(s_v.h, s_v.n + i - 1)
                    p = psWp[i % 2]
                    t.matmul(p[:, :], ab(A_LSELF + mt * R, R),
                             ab(A_WKU + ch * 512, 512),
                             start=True, stop=False)
                    t.matmul(p[:, :], rows_bf[0:1, mt * 128:(mt + 1) * 128],
                             rows_bf[0:1, 512 + ch * 512:512 + (ch + 1) * 512],
                             start=False, stop=True)
                    t.drain()
                    s_t.inc(t.nop())

        @b.vector
        def _(v):
            tbase = s_t.n - 8
            vbase = s_v.n
            for mt in range(2):
                for ch in range(4):
                    i = mt * 4 + ch
                    v.wait_ge(s_t.h, tbase + i + 1)
                    p = psWp[i % 2]
                    if ch < 2:
                        v.tensor_scalar(ab(A_KSEL + mt * D + ch * 512, 512),
                                        p[:, :], rsel_c[:, mt:mt + 1], None,
                                        op0=ALU.mult)
                    else:
                        base = A_VBUF + mt * 1088 + (ch - 2) * 8 * 65
                        v.tensor_scalar(
                            ABF[:, base:base + 520].rearrange(
                                "p (h e) -> p h e", e=65)[:, :, 0:64],
                            p[:, :].rearrange("p (h e) -> p h e", e=64),
                            rsel_c[:, mt:mt + 1], None, op0=ALU.mult)
                    v.drain()
                    s_v.inc(v.engine_nop())

    with nc.Block() as b:
        @b.vector
        def _(v):
            for mt in range(2):
                for blk in range(8):
                    par = blk ^ 1
                    v.tensor_tensor(ab(A_KTMP, 128),
                                    ab(A_KSEL + mt * D + blk * 128, 128),
                                    ab(A_COSS + mt * D + blk * 128, 128),
                                    ALU.mult)
                    v.tensor_tensor(ab(A_KTMP + 128, 128),
                                    ab(A_KSEL + mt * D + par * 128, 128),
                                    ab(A_SINS + mt * D + blk * 128, 128),
                                    ALU.mult)
                    v.drain()
                    v.tensor_tensor(ab(A_KROT + mt * D + blk * 128, 128),
                                    ab(A_KTMP, 128), ab(A_KTMP + 128, 128),
                                    ALU.add)
                    v.drain()

    with (
        nc.psum_tensor("psTa2", [128, 128], BF16) as psTa2,
        nc.psum_tensor("psTb2", [128, 128], BF16) as psTb2,
        nc.Block() as b,
    ):
        psTp = [psTa2, psTb2]

        @b.tensor
        def _(t):
            for i in range(16):
                mt, dt = i % 2, i // 2
                if i >= 2:
                    t.wait_ge(s_v.h, s_v.n + i - 1)
                t.transpose(psTp[i % 2][:, :],
                            ab(A_KROT + mt * D + dt * 128, 128), ident[:, :])
                t.drain()
                s_t.inc(t.nop())

        @b.vector
        def _(v):
            tbase = s_t.n - 16
            for i in range(16):
                mt, dt = i % 2, i // 2
                v.wait_ge(s_t.h, tbase + i + 1)
                v.tensor_copy(ab(A_KFM + dt * 256 + mt * 128, 128),
                              psTp[i % 2][:, :])
                v.drain()
                s_v.inc(v.engine_nop())

    with (
        nc.psum_tensor("psQB", [128, 512], F32) as psQB,
        nc.psum_tensor("psQB2", [128, 512], F32) as psQB2,
        nc.Block() as b,
    ):
        psQp = [psQB, psQB2]

        @b.tensor
        def _(t):
            t.wait_ge(s_dw.h, s_dw.n)
            for qc in range(4):
                if qc >= 2:
                    t.wait_ge(s_v.h, s_v.n + 2 * (qc - 1) + 1)
                t.matmul(psQp[qc % 2][:, :], onesrow_f[0:1, :],
                         rows[0:1, qc * 512:(qc + 1) * 512],
                         start=True, stop=True)
                t.drain()
                s_t.inc(t.nop())

        @b.vector
        def _(v):
            tbase = s_t.n - 4
            v.wait_ge(s_dw.h, s_dw.n)
            v.tensor_scalar(rows[96:97, 0:T], rows[0:1, 0:T],
                            constc[0:1, C_MINP:C_MINP + 1], -1.0,
                            op0=ALU.subtract, op1=ALU.mult)
            v.drain()
            v.tensor_scalar(rows[96:97, 0:T], rows[96:97, 0:T], 0.0, 1.0,
                            op0=ALU.max, op1=ALU.min)
            v.drain()
            v.tensor_copy(rows_bf[0:1, 0:T], rows[96:97, 0:T])
            v.drain()
            s_v.inc(v.engine_nop())
            for qc in range(4):
                v.wait_ge(s_t.h, tbase + qc + 1)
                for mt in range(2):
                    v.tensor_scalar(ab(A_MASK + mt * T + qc * 512, 512),
                                    psQp[qc % 2][:, :], pos_f[:, mt:mt + 1],
                                    None, op0=ALU.is_ge)
                    v.drain()
                    s_v.inc(v.engine_nop())

    with nc.Block() as b:
        @b.gpsimd
        def _(g):
            s_ff.inc(g.dma_start(
                ab(A_WQ, 8192).rearrange("p (dt o) -> p dt o", o=D),
                wqT_d.ap().rearrange("(dt p) o -> p dt o", p=128)), 16)

    with (
        nc.psum_tensor("psW3", [128, 512], F32) as psW3,
        nc.psum_tensor("psW4", [128, 512], F32) as psW4,
        nc.psum_tensor("psQ1", [1, 512], F32) as psQ1,
        nc.psum_tensor("psQ2", [1, 512], F32) as psQ2,
        nc.Block() as b,
    ):
        psWp = [psW3, psW4]

        psQw = [psQ1, psQ2]
        w1q_t0 = s_t.n

        @b.tensor
        def _(t):
            t.wait_ge(s_ff.h, s_ff.n)
            for ch in range(2):
                for d in range(8):
                    t.matmul(psQw[ch][:, :], onec_bf[:, :],
                             ab(A_WQ + d * D + ch * 512, 512),
                             start=(d == 0), stop=(d == 7),
                             skip_group_check=True)
                t.drain()
                s_t.inc(t.nop())

        @b.vector
        def _(v):
            for ch in range(2):
                v.wait_ge(s_t.h, w1q_t0 + 1 + ch)
                v.tensor_copy(rows_bf[0:1, 3600 + ch * 512:3600 + (ch + 1) * 512],
                              psQw[ch][:, :])
                v.drain()
                s_v.inc(v.engine_nop())

    with (
        nc.psum_tensor("psW5", [128, 512], F32) as psW5,
        nc.psum_tensor("psW6", [128, 512], F32) as psW6,
        nc.psum_tensor("psQ3", [1, 512], F32) as psQ3,
        nc.psum_tensor("psQ4", [1, 512], F32) as psQ4,
        nc.psum_tensor("psQ5", [1, 512], F32) as psQ5,
        nc.psum_tensor("psQ6", [1, 512], F32) as psQ6,
        nc.Block() as b,
    ):
        psWp = [psW5, psW6]
        psQ3p = [psQ3, psQ5]
        psQ4p = [psQ4, psQ6]
        t_ctr = [s_t.n]
        v_ctr = [s_v.n]
        s_ctr = [s_s.n]
        h_ctr = [s_dh.n]

        @b.gpsimd
        def _(g):
            for c in range(4):
                if c >= 1:
                    g.wait_ge(s_t.h, t_ctr[0] + c * 10)
                s_dh.inc(g.dma_start(
                    ab(A_XQH, 4096).rearrange("p (dt t) -> p dt t", t=512),
                    xTq_d.ap()[:, c * 512:(c + 1) * 512].rearrange(
                        "(dt p) t -> p dt t", p=128)), 16)

        @b.vector
        def _(v):
            for c in range(4):
                v.wait_ge(s_dh.h, h_ctr[0] + 16 * (c + 1))
                v.tensor_tensor(ab(A_XSQ, 4096), ab(A_XQH, 4096),
                                ab(A_XQH, 4096), ALU.mult)
                v.drain()
                s_v.inc(v.engine_nop())
                v.wait_ge(s_t.h, t_ctr[0] + c * 10 + 2)
                cs = slice(c * 512, (c + 1) * 512)
                mq = rows[0:1, 2048 + c * 512:2048 + (c + 1) * 512]
                tmp = constc[0:1, 416:928]
                v.tensor_copy(mq, psQ3p[c % 2][:, :])
                v.tensor_copy(rows[0:1, cs], psQ4p[c % 2][:, :])
                v.drain()
                v.tensor_scalar_mul(mq, mq, 1.0 / 1024.0)
                v.drain()
                v.tensor_tensor(tmp, mq, mq, ALU.mult)
                v.tensor_scalar_mul(rows[0:1, cs], rows[0:1, cs],
                                    1.0 / 1024.0)
                v.drain()
                v.tensor_tensor(rows[0:1, cs], rows[0:1, cs], tmp,
                                ALU.subtract)
                v.tensor_scalar_mul(rows_bf[0:1, 2048:2560], mq, -1.0)
                v.drain()
                s_v.inc(v.engine_nop())
                for mt in range(8):
                    v.wait_ge(s_t.h, t_ctr[0] + c * 10 + 3 + mt)
                    v.tensor_copy(ab(A_QBUF + mt * T + c * 512, 512),
                                  psWp[mt % 2][:, :])
                    v.drain()
                    s_v.inc(v.engine_nop())

        @b.scalar
        def _(s):
            for c in range(4):
                cs = slice(c * 512, (c + 1) * 512)
                s.wait_ge(s_v.h, v_ctr[0] + c * 10 + 2)
                s.activation(rows[0:1, cs], rows[0:1, cs], AF.Ln, bias=EPS)
                s.drain()
                s.activation(rows[0:1, cs], rows[0:1, cs], AF.Exp, scale=-0.5)
                s.drain()
                s_s.inc(s.nop())

        @b.tensor
        def _(t):
            for c in range(4):
                t.wait_ge(s_dh.h, h_ctr[0] + 16 * (c + 1))
                t.wait_ge(s_v.h, v_ctr[0] + c * 10 + 1)
                for d in range(8):
                    mm3 = t.matmul(psQ3p[c % 2][:, :], onec_bf[:, :],
                                   ab(A_XQH + d * 512, 512),
                                   start=(d == 0), stop=(d == 7))
                s_t.inc(mm3)
                for d in range(8):
                    mm4 = t.matmul(psQ4p[c % 2][:, :], onec_bf[:, :],
                                   ab(A_XSQ + d * 512, 512),
                                   start=(d == 0), stop=(d == 7))
                s_t.inc(mm4)
                for mt in range(8):
                    p = psWp[mt % 2]
                    if mt >= 2:
                        t.wait_ge(s_v.h, v_ctr[0] + c * 10 + 2 + (mt - 1))
                    for d in range(8):
                        t.matmul(p[:, :],
                                 ab(A_WQ + d * D + mt * 128, 128),
                                 ab(A_XQH + d * 512, 512),
                                 start=(d == 0), stop=False)
                    if mt == 0:
                        t.wait_ge(s_v.h, v_ctr[0] + c * 10 + 2)
                    mmq = t.matmul(
                        p[:, :],
                        rows_bf[0:1, 3600 + mt * 128:3600 + (mt + 1) * 128],
                        rows_bf[0:1, 2048:2560],
                        start=False, stop=True)
                    s_t.inc(mmq)

    with (
        nc.psum_tensor("psW7", [128, 512], F32) as psW7,
        nc.psum_tensor("psW8", [128, 512], F32) as psW8,
        nc.Block() as b,
    ):
        psWp = [psW7, psW8]
        t0 = [s_t.n]

        @b.tensor
        def _(t):
            for qc in range(4):
                if qc >= 2:
                    t.wait_ge(s_v.h, s_v.n + 3 * (qc - 1))
                t.matmul(psWp[qc % 2][:, :], onesrow_f[0:1, :],
                         rows[0:1, qc * 512:(qc + 1) * 512],
                         start=True, stop=True)
                t.drain()
                s_t.inc(t.nop())

        @b.vector
        def _(v):
            tb = s_t.n - 4
            for qc in range(4):
                v.wait_ge(s_t.h, tb + qc + 1)
                sl = slice(qc * 512, (qc + 1) * 512)
                v.tensor_tensor(ab(A_TABC, 2048)[:, sl], ab(A_FMTAB, 2048)[:, sl],
                                psWp[qc % 2][:, :], ALU.mult)
                v.tensor_tensor(ab(A_TABSP, 2048)[:, sl],
                                ab(A_FMTAB + 2048, 2048)[:, sl],
                                psWp[qc % 2][:, :], ALU.mult)
                v.tensor_tensor(ab(A_TABSN, 2048)[:, sl],
                                ab(A_FMTAB + 4096, 2048)[:, sl],
                                psWp[qc % 2][:, :], ALU.mult)
                v.drain()
                s_v.inc(v.engine_nop())
                v.drain()
                s_v.inc(v.engine_nop())
                v.drain()
                s_v.inc(v.engine_nop())

    with nc.Block() as b:
        @b.vector
        def _(v):
            for pp in range(4):
                q0 = ab(A_QBUF + (2 * pp) * T, T)
                q1 = ab(A_QBUF + (2 * pp + 1) * T, T)
                v.tensor_tensor(ab(A_QT1, T), q0, ab(A_TABC, T), ALU.mult)
                v.tensor_tensor(ab(A_QT2, T), q1, ab(A_TABSN, T), ALU.mult)
                v.tensor_tensor(ab(A_QT3, T), q1, ab(A_TABC, T), ALU.mult)
                v.tensor_tensor(ab(A_QT4, T), q0, ab(A_TABSP, T), ALU.mult)
                v.drain()
                v.tensor_tensor(q0, ab(A_QT1, T), ab(A_QT2, T), ALU.add)
                v.tensor_tensor(q1, ab(A_QT3, T), ab(A_QT4, T), ALU.add)
                v.drain()

    with (
        nc.psum_tensor("psS0", [128, 1024], F32) as psS0,
        nc.psum_tensor("psS1", [128, 1024], F32) as psS1,
        nc.psum_tensor("psPV0", [128, 1024], F32) as psPV0,
        nc.psum_tensor("psPV1", [128, 1024], F32) as psPV1,
        nc.Block() as b,
    ):
        psSp = [psS0, psS1]
        psPVp = [psPV0, psPV1]
        t0, v0, sc0 = s_t.n, s_v.n, s_s.n
        NST = 8

        @b.gpsimd
        def _(g):
            issue_precast(g, len(_pc_jobs))

        def tqk(qc, st, mt):
            return t0 + (qc * NST + st) * 3 + mt + 1

        def tpv(qc, st):
            return t0 + (qc * NST + st) * 3 + 3

        def sexp(qc, st, mt):
            return sc0 + (qc * NST + st) * 2 + mt + 1

        def vmsk(qc, st, mt):
            return v0 + (qc * NST + st) * 3 + mt + 1

        def vevac(qc, st):
            return v0 + (qc * NST + st) * 3 + 3

        @b.tensor
        def _(t):
            for qc in range(4):
                for st in range(NST):
                    gi = qc * NST + st
                    for mt in range(2):
                        par = (gi * 2 + mt) % 2
                        if gi * 2 + mt >= 2:
                            t.wait_ge(s_s.h, sc0 + gi * 2 + mt - 1)
                        for hh in range(2):
                            h = 2 * st + hh
                            j = h % 4
                            tp = h // 4
                            t.matmul(psSp[par][:, hh * 512:(hh + 1) * 512],
                                     ab(A_KFM + (2 * tp) * 256 + mt * 128, 128,
                                        32 * j, 32 * j + 32),
                                     ab(A_QBUF + (2 * tp) * T + qc * 512, 512,
                                        32 * j, 32 * j + 32),
                                     start=True, stop=False,
                                     tile_position=(32 * j, 0))
                            mmq = t.matmul(
                                psSp[par][:, hh * 512:(hh + 1) * 512],
                                ab(A_KFM + (2 * tp + 1) * 256 + mt * 128,
                                   128, 32 * j, 32 * j + 32),
                                ab(A_QBUF + (2 * tp + 1) * T + qc * 512,
                                   512, 32 * j, 32 * j + 32),
                                start=False, stop=True,
                                tile_position=(32 * j, 0))
                        s_t.inc(mmq)
                    pvpar = gi % 2
                    if gi >= 2:
                        t.wait_ge(s_v.h, vevac(qc, st - 2) if st >= 2 else
                                  vevac(qc - 1, st + NST - 2))
                    t.wait_ge(s_v.h, vmsk(qc, st, 1))
                    first_bank0 = True
                    first_bank1 = True
                    for hh in range(2):
                        h = 2 * st + hh
                        for qt in range(4):
                            col = (hh * 4 + qt) * 128
                            bank = col // 512
                            fb = first_bank0 if bank == 0 else first_bank1
                            for mt in range(2):
                                t.matmul(
                                    psPVp[pvpar][:, col:col + 65],
                                    ab(A_PBUF + ((gi % 2) * 2 + mt) * 1024 +
                                       hh * 512 + qt * 128, 128),
                                    ab(A_VBUF + mt * 1088 + h * 65, 65),
                                    start=(fb and mt == 0), stop=False,
                                    skip_group_check=True)
                                if bank == 0:
                                    first_bank0 = False
                                else:
                                    first_bank1 = False
                                fb = False
                            mmu = t.matmul(psPVp[pvpar][:, col:col + 65],
                                           rows_bf[0:1,
                                                   qc * 512 + qt * 128:
                                                   qc * 512 + qt * 128 + 128],
                                           rows_bf[0:1, 2560 + h * 65:
                                                   2560 + h * 65 + 65],
                                           start=False, stop=True,
                                           skip_group_check=True)
                    s_t.inc(mmu)

        @b.scalar
        def _(s):
            for qc in range(4):
                for st in range(NST):
                    gi = qc * NST + st
                    for mt in range(2):
                        par = (gi * 2 + mt) % 2
                        k = gi * 2 + mt
                        if k >= 2:
                            s.wait_ge(s_v.h, v0 + (gi - 1) * 3 + mt + 1)
                        s.wait_ge(s_t.h, tqk(qc, st, mt))
                        s_s.inc(s.activation(
                            ab(A_PTMP + par * 1024, 1024), psSp[par][:, :],
                            AF.Exp, scale=0.125))

        @b.vector
        def _(v):
            for qc in range(4):
                for st in range(NST):
                    gi = qc * NST + st
                    for mt in range(2):
                        par = (gi * 2 + mt) % 2
                        if gi >= 2:
                            v.wait_ge(s_t.h, tpv(*divmod(gi - 2, NST)))
                        v.wait_ge(s_s.h, sexp(qc, st, mt))
                        for hh in range(2):
                            tt = v.tensor_tensor(
                                ab(A_PBUF + ((gi % 2) * 2 + mt) * 1024 +
                                   hh * 512, 512),
                                ab(A_PTMP + par * 1024 + hh * 512, 512),
                                ab(A_MASK + mt * T + qc * 512, 512), ALU.mult)
                        s_v.inc(tt)
                    pvpar = gi % 2
                    v.wait_ge(s_t.h, tpv(qc, st))
                    v.reciprocal(recb[:, 0:8],
                                 psPVp[pvpar][:, :].rearrange(
                                     "p (a c) -> p a c", c=128)[:, :, 64:65])
                    v.drain()
                    for hh in range(2):
                        h = 2 * st + hh
                        for qt in range(4):
                            col = (hh * 4 + qt) * 128
                            ts = v.tensor_scalar(
                                ab(atm_off(qc * 4 + qt) + h * 64, 64),
                                psPVp[pvpar][:, col:col + 64],
                                recb[:, hh * 4 + qt:hh * 4 + qt + 1], None,
                                op0=ALU.mult)
                    s_v.inc(ts)

    with nc.Block() as b:
        @b.sync
        def _(sy):
            sy.wait_ge(s_pw.h, 32)
            s_dw.inc(sy.dma_start(ab(A_WOUT, 8192), wob_d.ap()), 16)

    with (
        nc.psum_tensor("psTc", [128, 512], BF16) as psTc,
        nc.psum_tensor("psTd", [128, 512], BF16) as psTd,
        nc.Block() as b,
    ):
        psTp = [psTc, psTd]
        t0, v0 = s_t.n, s_v.n
        afm_lo = ABF[:, 24704:24704 + 8192].rearrange("p (d c) -> p d c",
                                                      c=2048)
        afm_m = ABF[:, 24704 + 8192:24704 + 12288].rearrange(
            "p (d c) -> p d c", c=2048)
        afm_hi = ABF[:, 12288:16384].rearrange("p (d c) -> p d c", c=2048)

        @b.tensor
        def _(t):
            for g in range(32):
                qt, dtg = g // 2, g % 2
                par = g % 2
                if g >= 2:
                    t.wait_ge(s_v.h, v0 + g - 1)
                for k in range(4):
                    dt = dtg * 4 + k
                    tr = t.transpose(psTp[par][:, k * 128:(k + 1) * 128],
                                     ab(atm_off(qt) + dt * 128, 128),
                                     ident[:, :])
                s_t.inc(tr)

        @b.vector
        def _(v):
            for g in range(32):
                qt, dtg = g // 2, g % 2
                par = g % 2
                v.wait_ge(s_t.h, t0 + g + 1)
                src = psTp[par][:, :].rearrange("p (d c) -> p d c", c=128)
                if dtg == 0:
                    v.tensor_copy(afm_lo[:, :, qt * 128:qt * 128 + 128],
                                  src[:, 0:4, :])
                else:
                    v.tensor_copy(afm_m[:, :, qt * 128:qt * 128 + 128],
                                  src[:, 0:2, :])
                    v.tensor_copy(afm_hi[:, :, qt * 128:qt * 128 + 128],
                                  src[:, 2:4, :])
                v.drain()
                s_v.inc(v.engine_nop())

    def ln_rows(b, pm, cm, pr, cr, w, ln_scale=1.0):
        @b.vector
        def _(v):
            v.tensor_scalar_mul(rows[0:1, cm:cm + w],
                                rows[0:1, cm:cm + w], 1.0 / 1024.0)
            v.tensor_scalar_mul(rows[0:1, cr:cr + w],
                                rows[0:1, cr:cr + w], 1.0 / 1024.0)
            v.drain()
            for h0 in range(0, w, 1024):
                hw = min(1024, w - h0)
                v.tensor_tensor(AFX[0:1, 0:hw],
                                rows[0:1, cm + h0:cm + h0 + hw],
                                rows[0:1, cm + h0:cm + h0 + hw], ALU.mult)
                v.drain()
                v.tensor_tensor(rows[0:1, cr + h0:cr + h0 + hw],
                                rows[0:1, cr + h0:cr + h0 + hw],
                                AFX[0:1, 0:hw], ALU.subtract)
                v.drain()
                v.drain()
                s_v.inc(v.engine_nop())

        @b.scalar
        def _(s):
            s.wait_ge(s_v.h, s_v.n)
            s.activation(rows[0:1, cr:cr + w], rows[0:1, cr:cr + w],
                         AF.Ln, bias=EPS, scale=ln_scale * ln_scale)
            s.drain()
            s.activation(rows[0:1, cr:cr + w], rows[0:1, cr:cr + w],
                         AF.Exp, scale=-0.5)
            if ln_scale != 1.0:
                s.drain()
                s.mul(rows[0:1, cr:cr + w], rows[0:1, cr:cr + w], ln_scale)
            s.drain()
            s_s.inc(s.nop())

    with (
        nc.psum_tensor("psW9", [128, 512], F32) as psW9,
        nc.psum_tensor("psWa", [128, 512], F32) as psWa,
        nc.psum_tensor("psO1", [1, 512], F32) as psO1,
        nc.psum_tensor("psO3", [1, 512], F32) as psO3,
        nc.psum_tensor("psBm", [128, 512], F32) as psBm,
        nc.psum_tensor("psBr", [128, 512], F32) as psBr,
        nc.Block() as b,
    ):
        psWp = [psW9, psWa]
        t0, v0, sc0, g0 = s_t.n, s_v.n, s_s.n, s_g.n
        bcm, bcr = AFX[:, 1024:1536], AFX[:, 1536:2048]
        base_t = [0, 9, 19, 30]
        t_out = lambda qc, mt: base_t[qc] + mt + 1
        t_psO1 = [9, 18, 28, 39]
        t_psO3 = [19, 29, 40, 42]
        t_bc = [30, 41, 43, 44]
        base_v = [0, 9, 19, 31]
        v_evac = lambda qc, mt: base_v[qc] + mt + 1
        v_psO1c = [9, 18, 28, 40]
        v_pre = [19, 29, 41, 44]
        v_bcc = [30, 42, 45, 47]
        base_s = [0, 8, 17, 26]
        s_sq = lambda qc, dt: base_s[qc] + dt + 1
        s_fin = [17, 26, 35, 36]

        @b.tensor
        def _(t):
            def psO3_mms(qc):
                t.wait_ge(s_s.h, sc0 + s_sq(qc, 7))
                if qc >= 1:
                    t.wait_ge(s_v.h, v0 + v_pre[qc - 1])
                for dt in range(8):
                    mm = t.matmul(psO3[:, :], onec_bf[:, :],
                                  rows_bf[:, dt * 512:(dt + 1) * 512],
                                  start=(dt == 0), stop=(dt == 7))
                s_t.inc(mm)

            def bc_mms(qc):
                t.wait_ge(s_s.h, sc0 + s_fin[qc])
                if qc >= 1:
                    t.wait_ge(s_v.h, v0 + v_bcc[qc - 1])
                t.matmul(psBm[:, :], onesrow_f[0:1, :],
                         rows[0:1, qc * 512:(qc + 1) * 512],
                         start=True, stop=True)
                mm = t.matmul(psBr[:, :], onesrow_f[0:1, :],
                              rows[0:1, 2048 + qc * 512:2048 + (qc + 1) * 512],
                              start=True, stop=True)
                s_t.inc(mm)

            t.wait_ge(s_dw.h, s_dw.n)
            for qc in range(4):
                for mt in range(8):
                    if not (qc == 0 and mt < 2):
                        t.wait_ge(s_v.h, v0 + (v_evac(qc, mt - 2) if mt >= 2
                                               else v_evac(qc - 1, mt + 6)))
                    for d in range(8):
                        mm = t.matmul(psWp[mt % 2][:, :],
                                      ab(A_WOUT + d * D + mt * 128, 128),
                                      ab(afm_off(d) + qc * 512, 512),
                                      start=(d == 0), stop=(d == 7))
                    s_t.inc(mm)
                t.wait_ge(s_v.h, v0 + v_evac(qc, 7))
                if qc >= 1:
                    t.wait_ge(s_v.h, v0 + v_psO1c[qc - 1])
                for dt in range(8):
                    mm = t.matmul(psO1[:, :], onec_bf[:, :],
                                  ab(obuf_off(dt) + qc * 512, 512),
                                  start=(dt == 0), stop=(dt == 7))
                s_t.inc(mm)
                if qc >= 1:
                    psO3_mms(qc - 1)
                if qc >= 2:
                    bc_mms(qc - 2)
            psO3_mms(3)
            bc_mms(2)
            bc_mms(3)

        @b.scalar
        def _(s):
            for qc in range(4):
                for dt in range(8):
                    s.wait_ge(s_v.h, v0 + v_evac(qc, dt))
                    if dt == 0 and qc >= 1:
                        s.wait_ge(s_t.h, t0 + t_psO3[qc - 1])
                    s_s.inc(s.activation(
                        rows_bf[:, dt * 512:(dt + 1) * 512],
                        ab(obuf_off(dt) + qc * 512, 512), AF.Square))
                if qc >= 1:
                    qq = qc - 1
                    s.wait_ge(s_v.h, v0 + v_pre[qq])
                    sl = slice(2048 + qq * 512, 2048 + (qq + 1) * 512)
                    s.activation(rows[0:1, sl], rows[0:1, sl], AF.Ln,
                                 bias=EPS)
                    s.drain()
                    s.activation(rows[0:1, sl], rows[0:1, sl], AF.Exp,
                                 scale=-0.5)
                    s.drain()
                    s_s.inc(s.nop())
            qq = 3
            s.wait_ge(s_v.h, v0 + v_pre[qq])
            sl = slice(2048 + qq * 512, 2048 + (qq + 1) * 512)
            s.activation(rows[0:1, sl], rows[0:1, sl], AF.Ln, bias=EPS)
            s.drain()
            s.activation(rows[0:1, sl], rows[0:1, sl], AF.Exp, scale=-0.5)
            s.drain()
            s_s.inc(s.nop())

        def apply_tiles(eng, qc, dts, tmpa):
            for dt in dts:
                eng.tensor_tensor(tmpa, ab(obuf_off(dt) + qc * 512, 512),
                                  bcm, ALU.subtract)
                eng.drain()
                eng.tensor_tensor(tmpa, tmpa, bcr, ALU.mult)
                eng.drain()
                xs = xq[:, dt * T + qc * 512:dt * T + qc * 512 + 512]
                eng.tensor_tensor(xs, xs, tmpa, ALU.add)
                eng.drain()

        @b.vector
        def _(v):
            def pre_ops(qc):
                v.wait_ge(s_t.h, t0 + t_psO3[qc])
                ms = rows[0:1, qc * 512:(qc + 1) * 512]
                ss = rows[0:1, 2048 + qc * 512:2048 + (qc + 1) * 512]
                tmp = constc[0:1, 416:928]
                v.tensor_copy(ss, psO3[:, :])
                v.drain()
                v.tensor_scalar_mul(ms, ms, 1.0 / 1024.0)
                v.tensor_scalar_mul(ss, ss, 1.0 / 1024.0)
                v.drain()
                v.tensor_tensor(tmp, ms, ms, ALU.mult)
                v.drain()
                v.tensor_tensor(ss, ss, tmp, ALU.subtract)
                v.drain()
                s_v.inc(v.engine_nop())

            def bcc_apply(qc):
                v.wait_ge(s_t.h, t0 + t_bc[qc])
                if qc >= 1:
                    v.wait_ge(s_g.h, g0 + qc)
                v.tensor_copy(bcm, psBm[:, :])
                v.tensor_copy(bcr, psBr[:, :])
                v.drain()
                s_v.inc(v.engine_nop())
                apply_tiles(v, qc, range(0, 5), AFX[:, 0:512])
                s_v.inc(v.engine_nop())

            for qc in range(4):
                for mt in range(8):
                    v.wait_ge(s_t.h, t0 + t_out(qc, mt))
                    v.tensor_copy(ab(obuf_off(mt) + qc * 512, 512),
                                  psWp[mt % 2][:, :])
                    v.drain()
                    s_v.inc(v.engine_nop())
                v.wait_ge(s_t.h, t0 + t_psO1[qc])
                v.tensor_copy(rows[0:1, qc * 512:(qc + 1) * 512], psO1[:, :])
                v.drain()
                s_v.inc(v.engine_nop())
                if qc >= 1:
                    pre_ops(qc - 1)
                if qc >= 2:
                    bcc_apply(qc - 2)
            pre_ops(3)
            bcc_apply(2)
            bcc_apply(3)

        @b.gpsimd
        def _(g):
            for qc in range(4):
                g.wait_ge(s_v.h, v0 + v_bcc[qc])
                apply_tiles(g, qc, range(5, 8), AFX[:, 512:1024])
                s_g.inc(g.engine_nop())

    X2L_B = [A_X2L, A_X2L]
    DN_B = [A_DN, A_X2L]
    for tc in range(2):
        TBASE = tc * 1024
        CM3 = tc * 2048
        with (
            nc.psum_tensor(f"psF1_{tc}", [1, 512], F32) as psF1,
            nc.psum_tensor(f"psF2_{tc}", [1, 512], F32) as psF2,
            nc.Block() as b,
        ):
            t0, v0, g0 = s_t.n, s_v.n, s_g.n

            def ln3_pre(eng, lo, hi, sem):
                for dt in range(lo, hi):
                    eng.tensor_copy(ab(A_DN + dt * 1024, 1024),
                                    xq[:, dt * T + TBASE:dt * T + TBASE + 1024])
                eng.drain()
                eng.tensor_tensor(ab(A_HIDN + lo * 1024, (hi - lo) * 1024),
                                  ab(A_DN + lo * 1024, (hi - lo) * 1024),
                                  ab(A_DN + lo * 1024, (hi - lo) * 1024),
                                  ALU.mult)
                eng.drain()
                sem.inc(eng.engine_nop())

            @b.vector
            def _(v):
                ln3_pre(v, 0, 4, s_v)

            @b.gpsimd
            def _(g):
                ln3_pre(g, 4, 8, s_g)

            @b.tensor
            def _(t):
                t.wait_ge(s_v.h, v0 + 1)
                t.wait_ge(s_g.h, g0 + 1)
                for half in range(2):
                    for dt in range(8):
                        t.matmul(psF1[:, :], onec_bf[:, :],
                                 ab(A_DN + dt * 1024 + half * 512, 512),
                                 start=(dt == 0), stop=(dt == 7))
                    for dt in range(8):
                        t.matmul(psF2[:, :], onec_bf[:, :],
                                 ab(A_HIDN + dt * 1024 + half * 512, 512),
                                 start=(dt == 0), stop=(dt == 7))
                    t.drain()
                    s_t.inc(t.nop())
                    t.wait_ge(s_v.h, v0 + 1 + half + 1)

            @b.vector
            def _(v):
                for half in range(2):
                    v.wait_ge(s_t.h, t0 + half + 1)
                    sl = slice(CM3 + half * 512, CM3 + half * 512 + 512)
                    v.tensor_copy(rows[0:1, sl], psF1[:, :])
                    v.tensor_copy(rows[0:1, 1024 + sl.start:1024 + sl.stop],
                                  psF2[:, :])
                    v.drain()
                    s_v.inc(v.engine_nop())

        with nc.Block() as b:
            ln_rows(b, 0, CM3, 0, CM3 + 1024, 1024)

        if tc == 0:
            with (
                nc.psum_tensor(f"psBm3_{tc}", [128, 512], F32) as psBm3,
                nc.psum_tensor(f"psBr3_{tc}", [128, 512], F32) as psBr3,
                nc.Block() as b,
            ):
                t0, v0, g0 = s_t.n, s_v.n, s_g.n
                bcm3, bcr3 = AFX[:, 1024:1536], AFX[:, 1536:2048]

                @b.tensor
                def _(t):
                    for half in range(2):
                        if half >= 1:
                            t.wait_ge(s_v.h, v0 + 1)
                        sl = slice(CM3 + half * 512, CM3 + half * 512 + 512)
                        t.matmul(psBm3[:, :], onesrow_f[0:1, :],
                                 rows[0:1, sl], start=True, stop=True)
                        t.matmul(psBr3[:, :], onesrow_f[0:1, :],
                                 rows[0:1, 1024 + sl.start:1024 + sl.stop],
                                 start=True, stop=True)
                        t.drain()
                        s_t.inc(t.nop())

                def x2l_tiles(eng, half, dts, tmpa):
                    for dt in dts:
                        xs = xq[:, dt * T + TBASE + half * 512:
                                dt * T + TBASE + half * 512 + 512]
                        eng.tensor_tensor(tmpa, xs, bcm3, ALU.subtract)
                        eng.drain()
                        eng.tensor_tensor(
                            ab(X2L_B[tc] + dt * 1024 + half * 512, 512),
                            tmpa, bcr3, ALU.mult)
                        eng.drain()

                @b.vector
                def _(v):
                    for half in range(2):
                        v.wait_ge(s_t.h, t0 + half + 1)
                        if half >= 1:
                            v.wait_ge(s_g.h, g0 + 4)
                        v.tensor_copy(bcm3, psBm3[:, :])
                        v.tensor_copy(bcr3, psBr3[:, :])
                        v.drain()
                        s_v.inc(v.engine_nop())
                        x2l_tiles(v, half, range(0, 5), AFX[:, 0:512])
                        s_v.inc(v.engine_nop(), 4)

                @b.gpsimd
                def _(g):
                    for half in range(2):
                        g.wait_ge(s_v.h, v0 + half * 5 + 1)
                        x2l_tiles(g, half, range(5, 8), AFX[:, 512:1024])
                        s_g.inc(g.engine_nop(), 4)

    for tc in range(2):
        TBASE = tc * 1024
        with (
            nc.psum_tensor(f"psG0_{tc}", [128, 1024], F32) as psG0,
            nc.psum_tensor(f"psG1_{tc}", [128, 1024], F32) as psG1,
            nc.psum_tensor(f"psU0_{tc}", [128, 1024], F32) as psU0,
            nc.psum_tensor(f"psU1_{tc}", [128, 1024], F32) as psU1,
            nc.Block() as b,
        ):
            psGp = [psG0, psG1]
            psUp = [psU0, psU1]
            t0, v0, sc0 = s_t.n, s_v.n, s_s.n
            swp = [s_wa, s_wb]
            w0 = [s_wa.n, s_wb.n]
            silp = [ab(A_SIL, 1024), AFX[:, 0:1024]]
            upp = [ab(A_UP, 1024), AFX[:, 1024:2048]]

            @b.sync
            def _(sy):
                if tc == 0:
                    sy.wait_ge(s_pc.h, PC_TOTAL)
                for ht in range(24):
                    par = ht % 2
                    if ht >= 2:
                        sy.wait_ge(s_t.h, t0 + ht - 1)
                    swp[par].inc(sy.dma_start(ab(A_GU + par * 2048, 2048),
                                              wgu_d.ap()[ht]), 16)

            @b.tensor
            def _(t):
                for ht in range(24):
                    par = ht % 2
                    t.wait_ge(swp[par].h, w0[par] + 16 * (ht // 2 + 1))
                    if ht >= 2:
                        t.wait_ge(s_s.h, sc0 + 2 * (ht - 1))
                    for half in range(2):
                        for d in range(8):
                            t.matmul(psGp[par][:, half * 512:half * 512 + 512],
                                     ab(A_GU + par * 2048 + d * 128, 128),
                                     ab(X2L_B[tc] + d * 1024 + half * 512,
                                        512),
                                     start=(d == 0), stop=(d == 7))
                    for half in range(2):
                        for d in range(8):
                            mm = t.matmul(
                                psUp[par][:, half * 512:half * 512 + 512],
                                ab(A_GU + par * 2048 + 1024 + d * 128, 128),
                                ab(X2L_B[tc] + d * 1024 + half * 512, 512),
                                start=(d == 0), stop=(d == 7))
                    s_t.inc(mm)

            @b.scalar
            def _(s):
                for ht in range(24):
                    par = ht % 2
                    s.wait_ge(s_t.h, t0 + ht + 1)
                    if ht >= 2:
                        s.wait_ge(s_v.h, v0 + ht - 1)
                    s_s.inc(s.activation(silp[par], psGp[par][:, :],
                                         AF.Silu, scale=CFF))
                    s_s.inc(s.activation(upp[par], psUp[par][:, :],
                                         AF.Copy, scale=CFF))

            @b.vector
            def _(v):
                for ht in range(24):
                    par = ht % 2
                    v.wait_ge(s_s.h, sc0 + 2 * ht + 2)
                    v.tensor_tensor(ab(A_HIDN + ht * 1024, 1024),
                                    silp[par], upp[par], ALU.mult)
                    v.drain()
                    s_v.inc(v.engine_nop())

        NX = (tc == 0)
        with (
            nc.psum_tensor(f"psD0_{tc}", [128, 512], F32) as psD0,
            nc.psum_tensor(f"psD1_{tc}", [128, 512], F32) as psD1,
            nc.psum_tensor(f"psBp0_{tc}", [128, 512], F32) as psBp0,
            nc.psum_tensor(f"psBp1_{tc}", [128, 512], F32) as psBp1,
            nc.psum_tensor(f"psFp3_{tc}", [1, 512], F32) as psFp3,
            nc.psum_tensor(f"psFp4_{tc}", [1, 512], F32) as psFp4,
            nc.Block() as b,
        ):
            psDp = [psD0, psD1]
            psBm3d, psBr3d = psBp0, psBp1
            t0, v0, g0, sc0 = s_t.n, s_v.n, s_g.n, s_s.n
            swp = [s_wa, s_wb]
            w0 = [s_wa.n, s_wb.n]
            bcm3, bcr3 = AFX[:, 1024:1536], AFX[:, 1536:2048]
            wdap = [lambda k: ab(A_WD + k * 128, 128),
                    lambda k: rows_bf[:, k * 128:(k + 1) * 128]]
            wdfl = [ab(A_WD, 3072), rows_bf[:, 0:3072]]
            if NX:
                tmap = [1, 2, 3, 4, 5, 6, 7, 8, 9, 10, 11, 12, 13, 14, 15, 16]
                tmap = [x + 2 for x in tmap]
                vmap = list(range(5, 21))
            else:
                tmap = [1, 2, 3, 4, 6, 7, 9, 10, 12, 13, 15, 16, 18, 20, 22,
                        24]
                vmap = [1, 2, 3, 4, 5, 6, 7, 8, 10, 11, 12, 13, 14, 16, 19,
                        22]
                T_F4A, T_F4B, T_F3H0 = 5, 8, 11
                T_F4C, T_F4D, T_F3H1 = 14, 17, 19
                T_BC = [21, 23]
                V_PRE = [9, 15]
                V_BCC = [17, 20]
                V_X3 = [18, 21]
                S_SQ = [4, 8, 13, 17]
                S_FIN = [9, 18]

            def x2l_next(eng, half, dts, tmpa):
                for dt in dts:
                    xs = xq[:, dt * T + 1024 + half * 512:
                            dt * T + 1024 + half * 512 + 512]
                    eng.tensor_tensor(tmpa, xs, bcm3, ALU.subtract)
                    eng.drain()
                    eng.tensor_tensor(
                        ab(A_X2L + dt * 1024 + half * 512, 512),
                        tmpa, bcr3, ALU.mult)
                    eng.drain()

            def x3m_tiles(eng, half, dts, tmpa):
                for dt in dts:
                    eng.tensor_tensor(
                        tmpa, ab(A_DN + dt * 1024 + half * 512, 512),
                        bcm3, ALU.subtract)
                    eng.drain()
                    eng.tensor_tensor(tmpa, tmpa, bcr3, ALU.mult)
                    eng.drain()
                    xs = xq[:, dt * T + half * 512:dt * T + half * 512 + 512]
                    eng.tensor_tensor(xs, xs, tmpa, ALU.add)
                    eng.drain()

            @b.sync
            def _(sy):
                for mt in range(8):
                    par = mt % 2
                    if mt >= 2:
                        sy.wait_ge(s_t.h, t0 + tmap[2 * mt - 3])
                    swp[par].inc(sy.dma_start(wdfl[par],
                                              wdb_d.ap()[mt]), 16)
                if not NX:
                    sy.wait_ge(s_v.h, v0 + V_X3[1])
                    sy.wait_ge(s_g.h, g0 + 2)
                    s_y.inc(sy.dma_start(
                        y_d.ap()[0:512, 0:1024].rearrange(
                            "(dt p) t -> p dt t", p=128),
                        xq[:, :].rearrange("p (dt t) -> p dt t", t=T)[
                            :, 0:4, 0:1024]), 16)

            @b.tensor
            def _(t):
                def dn_group(i):
                    mt, half = i // 2, i % 2
                    par = mt % 2
                    if half == 0:
                        t.wait_ge(swp[par].h, w0[par] + 16 * (mt // 2 + 1))
                    if i >= 2:
                        t.wait_ge(s_v.h, v0 + vmap[i - 2])
                    for kt in range(24):
                        mm = t.matmul(
                            psDp[i % 2][:, :],
                            wdap[par](kt),
                            ab(A_HIDN + kt * 1024 + half * 512, 512),
                            start=(kt == 0), stop=(kt == 23))
                    s_t.inc(mm)

                if NX:
                    for half in range(2):
                        if half == 1:
                            t.wait_ge(s_v.h, v0 + 1)
                        sl = slice(2048 + half * 512, 2048 + half * 512 + 512)
                        t.matmul(psBm3d[:, :], onesrow_f[0:1, :],
                                 rows[0:1, sl], start=True, stop=True)
                        mm = t.matmul(
                            psBr3d[:, :], onesrow_f[0:1, :],
                            rows[0:1, 1024 + sl.start:1024 + sl.stop],
                            start=True, stop=True)
                        s_t.inc(mm)
                    for i in range(16):
                        dn_group(i)
                else:
                    def f4_batch(lo, first):
                        for k in range(4):
                            mm = t.matmul(psFp4[:, :], onec_bf[:, :],
                                          ab(A_SIL + k * 512, 512),
                                          start=(first and k == 0), stop=False,
                                          skip_group_check=True)
                        s_t.inc(mm)

                    def f3_half(half):
                        for dt in range(8):
                            mm = t.matmul(
                                psFp3[:, :], onec_bf[:, :],
                                ab(A_DN + dt * 1024 + half * 512, 512),
                                start=(dt == 0), stop=(dt == 7),
                                skip_group_check=True)
                        s_t.inc(mm)

                    def bc4(half):
                        t.wait_ge(s_s.h, sc0 + S_FIN[half])
                        if half == 1:
                            t.wait_ge(s_v.h, v0 + V_BCC[0])
                        sl = slice(half * 512, half * 512 + 512)
                        t.matmul(psBm3d[:, :], onesrow_f[0:1, :],
                                 rows[0:1, sl], start=True, stop=True)
                        mm = t.matmul(
                            psBr3d[:, :], onesrow_f[0:1, :],
                            rows[0:1, 1024 + sl.start:1024 + sl.stop],
                            start=True, stop=True)
                        s_t.inc(mm)

                    for i in range(4):
                        dn_group(i)
                    t.wait_ge(s_s.h, sc0 + S_SQ[0])
                    f4_batch(0, True)
                    dn_group(4)
                    dn_group(5)
                    t.wait_ge(s_s.h, sc0 + S_SQ[1])
                    f4_batch(4, False)
                    dn_group(6)
                    dn_group(7)
                    f3_half(0)
                    dn_group(8)
                    dn_group(9)
                    t.wait_ge(s_s.h, sc0 + S_SQ[2])
                    t.wait_ge(s_v.h, v0 + V_PRE[0])
                    f4_batch(0, True)
                    dn_group(10)
                    dn_group(11)
                    t.wait_ge(s_s.h, sc0 + S_SQ[3])
                    f4_batch(4, False)
                    dn_group(12)
                    f3_half(1)
                    dn_group(13)
                    bc4(0)
                    dn_group(14)
                    bc4(1)
                    dn_group(15)

            @b.scalar
            def _(s):
                if not NX:
                    def sq_batch(half, lo):
                        for k in range(4):
                            dt = lo + k
                            s_s.inc(s.activation(
                                ab(A_SIL + k * 512, 512),
                                ab(A_DN + dt * 1024 + half * 512, 512),
                                AF.Square))

                    def fin(half):
                        s.wait_ge(s_v.h, v0 + V_PRE[half])
                        sl = slice(1024 + half * 512, 1024 + half * 512 + 512)
                        s.activation(rows[0:1, sl], rows[0:1, sl], AF.Ln,
                                     bias=EPS, scale=CFF * CFF)
                        s.drain()
                        s.activation(rows[0:1, sl], rows[0:1, sl], AF.Exp,
                                     scale=-0.5)
                        s.drain()
                        s.mul(rows[0:1, sl], rows[0:1, sl], CFF)
                        s.drain()
                        s_s.inc(s.nop())

                    sq_batch(0, 0)
                    s.wait_ge(s_t.h, t0 + T_F4A)
                    sq_batch(0, 4)
                    fin(0)
                    s.wait_ge(s_t.h, t0 + T_F4B)
                    sq_batch(1, 0)
                    s.wait_ge(s_t.h, t0 + T_F4C)
                    sq_batch(1, 4)
                    fin(1)
                    s.wait_ge(s_v.h, v0 + V_X3[1])
                    s.wait_ge(s_g.h, g0 + 2)
                    s_y.inc(s.dma_start(
                        y_d.ap()[512:1024, 0:1024].rearrange(
                            "(dt p) t -> p dt t", p=128),
                        xq[:, :].rearrange("p (dt t) -> p dt t", t=T)[
                            :, 4:8, 0:1024]), 16)

            @b.vector
            def _(v):
                def dn_evac(i):
                    mt, half = i // 2, i % 2
                    v.wait_ge(s_t.h, t0 + tmap[i])
                    v.tensor_copy(
                        ab(DN_B[tc] + mt * 1024 + half * 512, 512),
                        psDp[i % 2][:, :])
                    v.drain()
                    s_v.inc(v.engine_nop())

                if NX:
                    for half in range(2):
                        v.wait_ge(s_t.h, t0 + half + 1)
                        if half == 1:
                            v.wait_ge(s_g.h, g0 + 1)
                        v.tensor_copy(bcm3, psBm3d[:, :])
                        v.tensor_copy(bcr3, psBr3d[:, :])
                        v.drain()
                        s_v.inc(v.engine_nop())
                        x2l_next(v, half, range(0, 5), AFX[:, 0:512])
                        s_v.inc(v.engine_nop())
                    for i in range(16):
                        dn_evac(i)
                else:
                    def pre(half):
                        v.wait_ge(s_t.h, t0 + (T_F3H0 if half == 0
                                               else T_F3H1))
                        ms = rows[0:1, half * 512:half * 512 + 512]
                        ss = rows[0:1, 1024 + half * 512:
                                  1024 + half * 512 + 512]
                        tmp = constc[0:1, 416:928]
                        v.tensor_copy(ms, psFp3[:, :])
                        v.tensor_copy(ss, psFp4[:, :])
                        v.drain()
                        v.tensor_scalar_mul(ms, ms, 1.0 / 1024.0)
                        v.tensor_scalar_mul(ss, ss, 1.0 / 1024.0)
                        v.drain()
                        v.tensor_tensor(tmp, ms, ms, ALU.mult)
                        v.drain()
                        v.tensor_tensor(ss, ss, tmp, ALU.subtract)
                        v.drain()
                        s_v.inc(v.engine_nop())

                    def bcc_x3(half):
                        v.wait_ge(s_t.h, t0 + T_BC[half])
                        if half == 1:
                            v.wait_ge(s_g.h, g0 + 1)
                        v.tensor_copy(bcm3, psBm3d[:, :])
                        v.tensor_copy(bcr3, psBr3d[:, :])
                        v.drain()
                        s_v.inc(v.engine_nop())
                        x3m_tiles(v, half, range(0, 5), AFX[:, 0:512])
                        s_v.inc(v.engine_nop())

                    for i in range(8):
                        dn_evac(i)
                    pre(0)
                    dn_evac(8)
                    dn_evac(9)
                    dn_evac(10)
                    dn_evac(11)
                    dn_evac(12)
                    pre(1)
                    dn_evac(13)
                    bcc_x3(0)
                    dn_evac(14)
                    bcc_x3(1)
                    dn_evac(15)

            @b.gpsimd
            def _(g):
                if NX:
                    for half in range(2):
                        g.wait_ge(s_v.h, v0 + half * 2 + 1)
                        x2l_next(g, half, range(5, 8), AFX[:, 512:1024])
                        s_g.inc(g.engine_nop())
                else:
                    for half in range(2):
                        g.wait_ge(s_v.h, v0 + V_BCC[half])
                        x3m_tiles(g, half, range(5, 8), AFX[:, 512:1024])
                        s_g.inc(g.engine_nop())

        if tc == 0:
            continue
        with (
            nc.psum_tensor(f"psF3_{tc}", [1, 512], F32) as psF3,
            nc.psum_tensor(f"psF4_{tc}", [1, 512], F32) as psF4,
            nc.Block() as b,
        ):
            t0, v0, sc0 = s_t.n, s_v.n, s_s.n

            @b.scalar
            def _(s):
                for half in range(2):
                    for dt in range(8):
                        i = half * 8 + dt
                        if i >= 2:
                            pdt, phalf = (dt - 2) % 8, half - (1 if dt < 2 else 0)
                            s.wait_ge(s_t.h, t0 + phalf * 9 + pdt + 1)
                        s_s.inc(s.activation(
                            ab(A_SIL + (i % 2) * 512, 512),
                            ab(DN_B[tc] + dt * 1024 + half * 512, 512),
                            AF.Square))

            @b.tensor
            def _(t):
                for half in range(2):
                    if half >= 1:
                        t.wait_ge(s_v.h, v0 + half)
                    for dt in range(8):
                        i = half * 8 + dt
                        t.wait_ge(s_s.h, sc0 + i + 1)
                        t.matmul(psF4[:, :], onec_bf[:, :],
                                 ab(A_SIL + (i % 2) * 512, 512),
                                 start=(dt == 0), stop=(dt == 7),
                                 skip_group_check=True)
                        t.drain()
                        s_t.inc(t.nop())
                    for dt in range(8):
                        t.matmul(psF3[:, :], onec_bf[:, :],
                                 ab(DN_B[tc] + dt * 1024 + half * 512, 512),
                                 start=(dt == 0), stop=(dt == 7),
                                 skip_group_check=True)
                    t.drain()
                    s_t.inc(t.nop())

            @b.vector
            def _(v):
                for half in range(2):
                    v.wait_ge(s_t.h, t0 + half * 9 + 9)
                    sl = slice(half * 512, half * 512 + 512)
                    v.tensor_copy(rows[0:1, sl], psF3[:, :])
                    v.tensor_copy(rows[0:1, 1024 + sl.start:1024 + sl.stop],
                                  psF4[:, :])
                    v.drain()
                    s_v.inc(v.engine_nop())

        with nc.Block() as b:
            ln_rows(b, 0, 0, 0, 1024, 1024, ln_scale=CFF)

        with (
            nc.psum_tensor(f"psBm4_{tc}", [128, 512], F32) as psBm4,
            nc.psum_tensor(f"psBr4_{tc}", [128, 512], F32) as psBr4,
            nc.Block() as b,
        ):
            t0, v0, g0 = s_t.n, s_v.n, s_g.n
            bcm4, bcr4 = AFX[:, 1024:1536], AFX[:, 1536:2048]

            @b.tensor
            def _(t):
                for half in range(2):
                    if half >= 1:
                        t.wait_ge(s_v.h, v0 + 1)
                    sl = slice(half * 512, half * 512 + 512)
                    t.matmul(psBm4[:, :], onesrow_f[0:1, :],
                             rows[0:1, sl], start=True, stop=True)
                    t.matmul(psBr4[:, :], onesrow_f[0:1, :],
                             rows[0:1, 1024 + sl.start:1024 + sl.stop],
                             start=True, stop=True)
                    t.drain()
                    s_t.inc(t.nop())

            def x3_tiles(eng, half, dts, tmpa):
                for dt in dts:
                    eng.tensor_tensor(
                        tmpa, ab(DN_B[tc] + dt * 1024 + half * 512, 512),
                        bcm4, ALU.subtract)
                    eng.drain()
                    eng.tensor_tensor(tmpa, tmpa, bcr4, ALU.mult)
                    eng.drain()
                    xs = xq[:, dt * T + TBASE + half * 512:
                            dt * T + TBASE + half * 512 + 512]
                    eng.tensor_tensor(xs, xs, tmpa, ALU.add)
                    eng.drain()

            @b.vector
            def _(v):
                for half in range(2):
                    v.wait_ge(s_t.h, t0 + half + 1)
                    if half >= 1:
                        v.wait_ge(s_g.h, g0 + 4)
                    v.tensor_copy(bcm4, psBm4[:, :])
                    v.tensor_copy(bcr4, psBr4[:, :])
                    v.drain()
                    s_v.inc(v.engine_nop())
                    x3_tiles(v, half, range(0, 4), AFX[:, 0:512])
                    s_v.inc(v.engine_nop(), 4)

            @b.gpsimd
            def _(g):
                for half in range(2):
                    g.wait_ge(s_v.h, v0 + half * 5 + 1)
                    x3_tiles(g, half, range(4, 8), AFX[:, 512:1024])
                    s_g.inc(g.engine_nop(), 4)

        with nc.Block() as b:
            @b.sync
            def _(sy):
                s_y.inc(sy.dma_start(
                    y_d.ap()[0:512, TBASE:TBASE + 1024].rearrange(
                        "(dt p) t -> p dt t", p=128),
                    xq[:, :].rearrange("p (dt t) -> p dt t", t=T)[
                        :, 0:4, TBASE:TBASE + 1024]), 16)

            @b.scalar
            def _(s):
                s_y.inc(s.dma_start(
                    y_d.ap()[512:1024, TBASE:TBASE + 1024].rearrange(
                        "(dt p) t -> p dt t", p=128),
                    xq[:, :].rearrange("p (dt t) -> p dt t", t=T)[
                        :, 4:8, TBASE:TBASE + 1024]), 16)

    with nc.Block() as b:
        @b.sync
        def _(sy):
            sy.wait_ge(s_y.h, s_y.n)

    ctx.close()
    return nc


_NC_CACHE = {}


def _get_nc():
    if "nc" not in _NC_CACHE:
        _NC_CACHE["nc"] = build_nc()
    return _NC_CACHE["nc"]


def _host_tables():
    if "tabs" in _NC_CACHE:
        return _NC_CACHE["tabs"]
    inv_freq = 1.0 / ROPE_BASE ** (np.arange(0, 64, 2, dtype=np.float32) / 64)
    pos = np.arange(S, dtype=np.float32)
    fr = pos[:, None] * inv_freq[None, :]
    cos_t = np.cos(fr).astype(np.float32)
    sin_t = np.sin(fr).astype(np.float32)
    cos_fm = np.ascontiguousarray(np.tile(cos_t.T, (4, 1)))
    sin_fm = np.ascontiguousarray(np.tile(sin_t.T, (4, 1)))
    cos_tm = np.ascontiguousarray(np.tile(cos_t, (1, 32))).astype(BF)
    sgn = np.where((np.arange(D) // 128) % 2 == 0, -1.0, 1.0).astype(np.float32)
    sin_tm = np.ascontiguousarray(np.tile(sin_t, (1, 32)) * sgn[None, :]) \
        .astype(BF)
    _NC_CACHE["tabs"] = (cos_fm, sin_fm, cos_tm, sin_tm)
    return _NC_CACHE["tabs"]


def kernel(L0_attn_x, L0_attn_w_q, L0_attn_w_kv_down, L0_attn_w_kv_up,
           L0_attn_w_out, L0_attn_w_scorer, L0_ffn_w_gate, L0_ffn_w_up,
           L0_ffn_w_down):
    x = np.ascontiguousarray(np.asarray(L0_attn_x, dtype=np.float32))
    w_q = np.asarray(L0_attn_w_q, dtype=np.float32)
    w_kvd = np.asarray(L0_attn_w_kv_down, dtype=np.float32)
    w_kvu = np.asarray(L0_attn_w_kv_up, dtype=np.float32)
    w_out = np.asarray(L0_attn_w_out, dtype=np.float32)
    w_sc = np.asarray(L0_attn_w_scorer, dtype=np.float32)
    w_g = np.asarray(L0_ffn_w_gate, dtype=np.int32)
    w_u = np.asarray(L0_ffn_w_up, dtype=np.int32)
    w_d = np.asarray(L0_ffn_w_down, dtype=np.int32)

    cos_fm, sin_fm, cos_tm, sin_tm = _host_tables()

    wqT = np.ascontiguousarray(w_q.T[:, PERM])
    wkvdT = np.ascontiguousarray(w_kvd.T)
    wkuT_full = w_kvu.T
    wkuT = np.ascontiguousarray(
        np.concatenate([wkuT_full[:, 0:D][:, PERM], wkuT_full[:, D:2 * D]],
                       axis=1))
    woutT = np.ascontiguousarray(w_out.T)
    wscT = np.ascontiguousarray(w_sc.reshape(8, 128).T)
    wgT = np.ascontiguousarray(w_g.T)
    wuT = np.ascontiguousarray(w_u.T)
    wdT = np.ascontiguousarray(w_d.T)

    shared = {
        "wqT": wqT, "wkvdT": wkvdT, "wkuT": wkuT, "woutT": woutT,
        "wscT": wscT, "wgT": wgT, "wuT": wuT, "wdT": wdT,
        "costm": cos_tm, "sintm": sin_tm,
    }
    in_maps = []
    for core in range(8):
        bidx, hf = core // 2, core % 2
        xTb = np.ascontiguousarray(x[bidx].T)
        qoff = hf * T
        m = dict(shared)
        m["xT"] = xTb
        m["xTq"] = np.ascontiguousarray(xTb[:, qoff:qoff + T])
        m["xtm"] = x[bidx]
        m["qiota"] = np.arange(qoff, qoff + T, dtype=np.float32).reshape(1, T)
        m["cosfm"] = np.ascontiguousarray(cos_fm[:, qoff:qoff + T]).astype(BF)
        m["sinpfm"] = np.ascontiguousarray(sin_fm[:, qoff:qoff + T]).astype(BF)
        m["sinnfm"] = np.ascontiguousarray(-sin_fm[:, qoff:qoff + T]).astype(BF)
        in_maps.append(m)

    nc = _get_nc()
    res = run_bass_kernel_spmd(nc, in_maps, core_ids=list(range(8)))

    out = np.empty((4, S, D), dtype=np.float32)
    for core in range(8):
        bidx, hf = core // 2, core % 2
        out[bidx, hf * T:(hf + 1) * T, :] = res.results[core]["y"].T
    return out


# version 16 (final)
# speedup vs baseline: 1.3521x; 1.0235x over previous
import contextlib
import numpy as np
import ml_dtypes

import concourse.bass as bass
import concourse.mybir as mybir
from concourse.bass_utils import run_bass_kernel_spmd
from concourse.bass_core_rust import IndirectOffsetOnAxis

F32 = mybir.dt.float32
BF16 = mybir.dt.bfloat16
I32 = mybir.dt.int32
U32 = mybir.dt.uint32
AF = mybir.ActivationFunctionType
ALU = mybir.AluOpType
AX = mybir.AxisListType

D = 1024
S = 4096
T = 2048
R = 128
EPS = 1e-5
HID = 3072
CFF = float(np.float32(1.0 / 127.0))
ROPE_BASE = 100000.0
BF = ml_dtypes.bfloat16

PERM = []
for _t in range(4):
    for _h in range(4 * _t, 4 * _t + 4):
        PERM.extend(range(64 * _h, 64 * _h + 32))
    for _h in range(4 * _t, 4 * _t + 4):
        PERM.extend(range(64 * _h + 32, 64 * _h + 64))
PERM = np.array(PERM)

RX_SX, RX_P1, RX_P2 = 1, 2, 3
RX_SXX = 4
RX_M, RX_RS, RX_TMP, RX_TMP2 = 5, 6, 7, 8
RX_STOK = 9
RX_SXLO, RX_P3 = 10, 11
RX_MQ, RX_SQ1, RX_SQ2 = 12, 13, 14
RX_BLK, RX_IOTA64, RX_SCAL = 15, 16, 18
RX_U, RX_T8V, RX_BLK4 = 20, 21, 22

A_LAT = 0
A_XHI = 4096
A_XLO = 8192
A_X2B = 10240
A_COSS = 4096
A_SINS = 6144
A_KSEL = 8192
A_KROT = 10240
A_KFM = 12288
A_VBUF = 14336
A_MASK = 16512
A_WKVD = 12288
A_WKU = 21376
A_LSEL = 20608
A_LSELF = 20864
A_KTMP = 21120
A_WQ = 0
A_XQH = 8192
A_XSQ = 36992
A_QBUF = 20608
A_TABC = 36992
A_TABSP = 39040
A_TABSN = 41088
A_FMTAB = 43136
A_QT1, A_QT2, A_QT3, A_QT4 = 43136, 45184, 47232, 8192
A_PBUF = 0
A_PTMP = 4096
A_WOUT = 16512
A_OBUF = 36992
A_X2L = 0
A_GU = 8192
A_WD = 12288
A_SIL = 15360
A_UP = 16384
A_HIDN = 17408
A_DN = 41984
ABF_SZ = 50176


def atm_off(qtile):
    return 6144 + qtile * 1024 if qtile < 6 else 36992 + (qtile - 6) * 1024


def afm_off(dt):
    return 24704 + dt * 2048 if dt < 6 else 12288 + (dt - 6) * 2048


def obuf_off(mt):
    return 36992 + mt * 2048 if mt < 6 else (mt - 6) * 2048


def build_nc():
    nc = bass.Bass("TRN2", target_bir_lowering=False)

    xT_d = nc.dram_tensor("xT", [D, S], F32, kind="ExternalInput")
    xTq_d = nc.dram_tensor("xTq", [D, T], F32, kind="ExternalInput")
    xtm_d = nc.dram_tensor("xtm", [S, D], F32, kind="ExternalInput")
    qiota_d = nc.dram_tensor("qiota", [1, T], F32, kind="ExternalInput")
    wqT_d = nc.dram_tensor("wqT", [D, D], F32, kind="ExternalInput")
    wkvdT_d = nc.dram_tensor("wkvdT", [D, R], F32, kind="ExternalInput")
    wkuT_d = nc.dram_tensor("wkuT", [R, 2 * D], F32, kind="ExternalInput")
    woutT_d = nc.dram_tensor("woutT", [D, D], F32, kind="ExternalInput")
    wscT_d = nc.dram_tensor("wscT", [128, 8], F32, kind="ExternalInput")
    wgT_d = nc.dram_tensor("wgT", [D, HID], I32, kind="ExternalInput")
    wuT_d = nc.dram_tensor("wuT", [D, HID], I32, kind="ExternalInput")
    wdT_d = nc.dram_tensor("wdT", [HID, D], I32, kind="ExternalInput")
    cosfm_d = nc.dram_tensor("cosfm", [128, T], BF16, kind="ExternalInput")
    sinpfm_d = nc.dram_tensor("sinpfm", [128, T], BF16, kind="ExternalInput")
    sinnfm_d = nc.dram_tensor("sinnfm", [128, T], BF16, kind="ExternalInput")
    costm_d = nc.dram_tensor("costm", [S, D], BF16, kind="ExternalInput")
    sintm_d = nc.dram_tensor("sintm", [S, D], BF16, kind="ExternalInput")
    y_d = nc.dram_tensor("y", [D, T], F32, kind="ExternalOutput")

    lat_dram = nc.dram_tensor("lat_dram", [S, R], BF16, kind="Internal")
    wgu_d = nc.dram_tensor("wgu", [24, 128, 2048], BF16, kind="Internal")
    wdb_d = nc.dram_tensor("wdb", [8, 128, 3072], BF16, kind="Internal")
    wob_d = nc.dram_tensor("wob", [128, 8192], BF16, kind="Internal")

    ctx = contextlib.ExitStack()
    SB = lambda n, s, d: ctx.enter_context(nc.sbuf_tensor(n, s, d))

    rows = SB("rows", [128, S], F32)
    xq = SB("xq", [128, 8 * T], F32)
    rows_bf = SB("rows_bf", [128, 4640], BF16)
    AFX = SB("AFX", [128, 2048], F32)
    ABF = SB("ABF", [128, ABF_SZ], BF16)
    ident = SB("ident", [128, 128], BF16)
    identf = SB("identf", [128, 128], F32)
    onec_bf = SB("onec_bf", [128, 1], BF16)
    onec_f = SB("onec_f", [128, 1], F32)
    onesrow_f = SB("onesrow_f", [128, 128], F32)
    one1_f = SB("one1_f", [1, 1], F32)
    PAN = SB("PAN", [128, 8 * 32], F32)
    kvx = SB("kvx", [128, 8 * 131], BF16)
    statB = SB("statB", [128, 8 * 2], BF16)
    halfon = SB("halfon", [128, 2], F32)
    csc_col = SB("csc_col", [128, 1], F32)
    blk2 = SB("blk2", [2, 32], F32)
    wsc_f = SB("wsc_f", [128, 8], F32)
    idx8 = SB("idx8", [1, 8], U32)
    idx_u32 = SB("idx_u32", [128, 2], U32)
    pos_f = SB("pos_f", [128, 2], F32)
    msel_c = SB("msel_c", [128, 2], F32)
    rsel_c = SB("rsel_c", [128, 2], F32)
    rcols = SB("rcols", [128, 32], BF16)
    mlat_c = SB("mlat_c", [128, 1], BF16)
    w1d_c = SB("w1d_c", [128, 1], BF16)
    recb = SB("recb", [128, 8], F32)
    constc = SB("constc", [128, 928], F32)
    _CONSTS = [EPS, 0.125, -0.5, CFF, CFF * CFF, 1.0, -1.0, 64.0]
    for _i, _cv in enumerate(_CONSTS):
        nc.const_aps.register(F32, _cv, constc[:, _i:_i + 1])

    def ab(off, w, p0=0, p1=128):
        return ABF[p0:p1, off:off + w]

    class SemC:
        def __init__(self, name):
            self.h = ctx.enter_context(nc.semaphore(name))
            self.n = 0

        def inc(self, inst, k=1):
            self.n += k
            return inst.then_inc(self.h, k)

    s_dx = SemC("s_dx")
    s_dw = SemC("s_dw")
    s_dh = SemC("s_dh")
    s_v = SemC("s_v")
    s_s = SemC("s_s")
    s_t = SemC("s_t")
    s_out = SemC("s_out")
    s_gat = SemC("s_gat")
    s_ff = SemC("s_ff")
    s_y = SemC("s_y")
    s_pc = SemC("s_pc")
    s_pw = SemC("s_pw")
    s_wa = SemC("s_wa")
    s_wb = SemC("s_wb")
    s_g = SemC("s_g")
    s_dz = SemC("s_dz")

    _pc_jobs = []
    for _ht in range(24):
        _pc_jobs.append((
            wgu_d.ap()[_ht][:, 0:1024].rearrange("p (dt c) -> p dt c", c=128),
            wgT_d.ap()[:, _ht * 128:(_ht + 1) * 128].rearrange(
                "(dt p) c -> p dt c", p=128)))
        _pc_jobs.append((
            wgu_d.ap()[_ht][:, 1024:2048].rearrange("p (dt c) -> p dt c",
                                                    c=128),
            wuT_d.ap()[:, _ht * 128:(_ht + 1) * 128].rearrange(
                "(dt p) c -> p dt c", p=128)))
    for _mt in range(8):
        _pc_jobs.append((
            wdb_d.ap()[_mt].rearrange("p (kt c) -> p kt c", c=128),
            wdT_d.ap()[:, _mt * 128:(_mt + 1) * 128].rearrange(
                "(kt p) c -> p kt c", p=128)))
    PC_TOTAL = 16 * len(_pc_jobs)
    _pc_state = [0]

    def issue_precast(g, njobs):
        while njobs > 0 and _pc_state[0] < len(_pc_jobs):
            i = _pc_state[0]
            dst, src = _pc_jobs[i]
            s_pc.inc(g.dma_start(dst, src), 16)
            _pc_state[0] = i + 1
            njobs -= 1

    with nc.Block() as b:
        @b.gpsimd
        def _(g):
            for _i, _cv in enumerate(_CONSTS):
                g.memset(constc[:, _i:_i + 1], _cv)
            g.memset(onec_bf[:, :], 1.0)
            g.memset(onec_f[:, :], 1.0)
            g.memset(one1_f[:, :], 1.0)
            g.iota(constc[0:1, 8:8 + 64], [[1, 64]],
                   channel_multiplier=0, allow_small_or_imprecise_dtypes=True)
            s_ff.inc(g.dma_start(
                kvx[:, :].rearrange("p (dt c) -> p dt c", c=131)[:, :, 0:128],
                wkvdT_d.ap().rearrange("(dt p) r -> p dt r", p=128)), 16)
            s_ff.inc(g.dma_start(ab(A_WKU, 2048), wkuT_d.ap()), 16)

        @b.vector
        def _(v):
            v.memset(onesrow_f[:, :], 1.0)
            v.memset(ident[:, :], 1.0)
            v.memset(identf[:, :], 1.0)
            v.memset(ab(A_VBUF, 2176), 1.0)

        @b.sync
        def _(sy):
            s_dw.inc(sy.dma_start(wsc_f[:, :], wscT_d.ap()), 16)
            s_dw.inc(sy.dma_start(ab(A_FMTAB, 2048), cosfm_d.ap()), 16)
            s_dw.inc(sy.dma_start(ab(A_FMTAB + 2048, 2048), sinpfm_d.ap()), 16)
            s_dw.inc(sy.dma_start(ab(A_FMTAB + 4096, 2048), sinnfm_d.ap()), 16)
            s_y.inc(sy.dma_start(
                xq[:, :].rearrange("p (dt t) -> p dt t", t=T),
                xTq_d.ap().rearrange("(dt p) t -> p dt t", p=128)), 16)

    def kvx_prologue(v):
        v.wait_ge(s_ff.h, s_ff.n)
        v.wait_ge(s_dw.h, s_dw.n)
        v.memset(halfon[:, :], 0.0)
        v.drain()
        v.memset(halfon[0:64, 0:1], 1.0)
        v.memset(halfon[64:128, 1:2], 1.0)
        for d in range(8):
            v.memset(kvx[:, 131 * d + 128:131 * d + 129], 1.0)
            v.memset(statB[:, 2 * d:2 * d + 1], 1.0)
            v.tensor_copy(kvx[:, 131 * d + 129:131 * d + 130],
                          wsc_f[:, d:d + 1])
            v.tensor_copy(statB[:, 2 * d + 1:2 * d + 2], wsc_f[:, d:d + 1])
        v.drain()
        for d in range(8):
            v.tensor_tensor(kvx[:, 131 * d + 130:131 * d + 131],
                            wsc_f[:, d:d + 1],
                            kvx[:, 131 * d + 129:131 * d + 130],
                            ALU.subtract)
        v.drain()
        s_v.inc(v.engine_nop())

    NCH, CW = 32, 128
    with (
        nc.psum_tensor("psL0", [128, 131], F32) as psL0,
        nc.psum_tensor("psL1", [128, 131], F32) as psL1,
        nc.psum_tensor("psP0", [128, 3], F32) as psP0,
        nc.psum_tensor("psP1", [128, 3], F32) as psP1,
        nc.Block() as b,
    ):
        psL = [psL0, psL1]
        psS = [psP0, psP1]

        def xck(c, d=None):
            if c % 3 == 2:
                base = rows
                off = 0
            else:
                base = AFX
                off = (c % 3) * 1024
            if d is None:
                return base[:, off:off + 1024]
            return base[:, off + d * CW:off + (d + 1) * CW]

        def xhck(c, d=None):
            off = A_XHI + (c % 3) * 1024 if c % 3 < 2 else 6144
            if d is None:
                return ab(off, 1024)
            return ab(off + d * CW, CW)

        def xlck(c, d=None):
            off = A_XLO + (c % 3) * 1024 if c % 3 < 2 else 12288
            if d is None:
                return ab(off, 1024)
            return ab(off + d * CW, CW)

        def x2ck(c, d=None):
            off = A_X2B + (c % 3) * 1024 if c % 3 < 2 else 13312
            if d is None:
                return ab(off, 1024)
            return ab(off + d * CW, CW)

        def xsrc(c):
            return xT_d.ap()[:, c * CW:(c + 1) * CW].rearrange(
                "(dt p) t -> p dt t", p=128)

        def xwait(eng, c):
            sem = [s_dx, s_dh, s_dz][c % 3]
            eng.wait_ge(sem.h, 16 * (c // 3 + 1))

        @b.sync
        def _(sy):
            for c in range(0, NCH, 3):
                if c >= 3:
                    sy.wait_ge(s_v.h, 2 * (c - 3) + 2)
                    sy.wait_ge(s_s.h, 2 * (c - 3) + 2)
                s_dx.inc(sy.dma_start(
                    xck(c).rearrange("p (dt t) -> p dt t", t=CW), xsrc(c)), 16)

        @b.gpsimd
        def _(g):
            g.affine_select(ident[:, :], ident[:, :], [[1, 128]], ALU.is_equal,
                            0.0, channel_multiplier=-1)
            g.affine_select(identf[:, :], identf[:, :], [[1, 128]],
                            ALU.is_equal, 0.0, channel_multiplier=-1)
            for c in range(NCH):
                if c % 3 == 1:
                    if c >= 3:
                        g.wait_ge(s_v.h, 2 * (c - 3) + 2)
                        g.wait_ge(s_s.h, 2 * (c - 3) + 2)
                    s_dh.inc(g.dma_start(
                        xck(c).rearrange("p (dt t) -> p dt t", t=CW),
                        xsrc(c)), 16)
                if c >= 2:
                    g.wait_ge(s_v.h, 2 * (c - 2) + 3)
                    s_out.inc(g.dma_start(
                        lat_dram.ap()[(c - 2) * CW:(c - 1) * CW, :],
                        ab(A_LAT + (c - 2) * R, R)), 16)
            for c in (NCH - 2, NCH - 1):
                g.wait_ge(s_v.h, 2 * c + 3)
                s_out.inc(g.dma_start(
                    lat_dram.ap()[c * CW:(c + 1) * CW, :],
                    ab(A_LAT + c * R, R)), 16)

        @b.scalar
        def _(s):
            s_dz.inc(s.dma_start(
                xck(2).rearrange("p (dt t) -> p dt t", t=CW), xsrc(2)), 16)
            for c in range(NCH):
                xwait(s, c)
                if c >= 3:
                    s.wait_ge(s_t.h, 2 * (c - 3) + 2)
                    s.wait_ge(s_v.h, 2 * (c - 3) + 2)
                s_s.inc(s.activation(xhck(c), xck(c), AF.Copy))
                s_s.inc(s.activation(x2ck(c), xck(c), AF.Square))
                cf = c + 3
                if cf < NCH and cf % 3 == 2:
                    s.wait_ge(s_v.h, 2 * c + 2)
                    s.drain()
                    s_dz.inc(s.dma_start(
                        xck(cf).rearrange("p (dt t) -> p dt t", t=CW),
                        xsrc(cf)), 16)

        @b.vector
        def _(v):
            kvx_prologue(v)
            for c in range(NCH):
                xwait(v, c)
                v.wait_ge(s_s.h, 2 * c + 1)
                if c >= 3:
                    v.wait_ge(s_t.h, 2 * (c - 3) + 2)
                s_v.inc(v.tensor_tensor(xlck(c), xck(c), xhck(c),
                                        ALU.subtract))
                v.wait_ge(s_t.h, 2 * c + 2)
                v.tensor_copy(ab(A_LAT + c * R, R), psL[c % 2][:, 0:128])
                v.tensor_copy(
                    PAN[:, :].rearrange("p (q w) -> p q w", w=32)[:, 0:3,
                                                                  c:c + 1],
                    psL[c % 2][:, 128:131].rearrange("p (a q) -> p q a", a=1))
                v.tensor_copy(
                    PAN[:, :].rearrange("p (q w) -> p q w", w=32)[:, 3:6,
                                                                  c:c + 1],
                    psS[c % 2][:, 0:3].rearrange("p (a q) -> p q a", a=1))
                v.drain()
                s_v.inc(v.engine_nop())

        @b.tensor
        def _(t):
            for c in range(NCH):
                t.wait_ge(s_s.h, 2 * c + 1)
                if c == 0:
                    t.wait_ge(s_v.h, 1)
                if c >= 2:
                    t.wait_ge(s_v.h, 2 * (c - 2) + 3)
                for d in range(8):
                    mm = t.matmul(psL[c % 2][:, 0:131], xhck(c, d),
                                  kvx[:, 131 * d:131 * (d + 1)],
                                  start=(d == 0), stop=(d == 7))
                s_t.inc(mm)
                t.wait_ge(s_v.h, 2 * c + 2)
                t.wait_ge(s_s.h, 2 * c + 2)
                for d in range(8):
                    t.matmul(psS[c % 2][:, 0:2], xlck(c, d),
                             statB[:, 2 * d:2 * d + 2],
                             start=(d == 0), stop=False)
                for d in range(8):
                    mm = t.matmul(psS[c % 2][:, 2:3], x2ck(c, d),
                                  onec_bf[:, :],
                                  start=False, stop=(d == 7))
                s_t.inc(mm)

    C_IOTA, C_OFF, C_CSC, C_MINP, C_SMR, C_BLK, C_BLK4, C_T8V = \
        8, 72, 328, 329, 330, 331, 395, 399
    PANv = PAN[:, :].rearrange("p (q w) -> p q w", w=32)

    def pq(q):
        return PAN[:, 32 * q:32 * (q + 1)]

    with (
        nc.psum_tensor("psRowA", [1, 512], F32) as psRowA,
        nc.psum_tensor("psColA", [128, 1], F32) as psColA,
        nc.psum_tensor("psColB", [128, 1], F32) as psColB,
        nc.psum_tensor("psBLK", [2, 32], F32) as psBLK,
        nc.psum_tensor("psSMR", [1, 32], F32) as psSMR,
        nc.Block() as b,
    ):
        @b.tensor
        def _(t):
            for d in range(8):
                t.matmul(psRowA[:, 0:1], wsc_f[:, d:d + 1], onec_f[:, :],
                         start=(d == 0), stop=(d == 7))
            for d in range(8):
                t.matmul(psRowA[:, 128:128 + R], onec_bf[:, :],
                         kvx[:, 131 * d:131 * d + 128],
                         start=(d == 0), stop=(d == 7), skip_group_check=True)
            for d in range(8):
                t.matmul(psColA[:, :], kvx[:, 131 * d:131 * d + 128],
                         onec_bf[:, :], start=(d == 0), stop=(d == 7))
            t.drain()
            s_t.inc(t.nop())

        @b.vector
        def _(v):
            v.wait_ge(s_t.h, s_t.n)
            v.tensor_copy(constc[0:1, C_CSC:C_CSC + 1], psRowA[:, 0:1])
            v.tensor_copy(rows_bf[0:1, 256:256 + R], psRowA[:, 128:128 + R])
            v.tensor_copy(w1d_c[:, :], psColA[:, :])
            v.tensor_tensor(pq(6), pq(0), pq(3), ALU.add)
            v.drain()
            v.tensor_scalar_mul(pq(6), pq(6), 1.0 / 1024.0)
            v.tensor_scalar_mul(pq(5), pq(5), 1.0 / 1024.0)
            v.drain()
            v.tensor_tensor(pq(0), pq(6), pq(6), ALU.mult)
            v.drain()
            v.tensor_tensor(pq(5), pq(5), pq(0), ALU.subtract)
            v.drain()
            s_v.inc(v.engine_nop())

        @b.scalar
        def _(s):
            s.wait_ge(s_v.h, s_v.n)
            s.activation(pq(7), pq(5), AF.Ln, bias=EPS)
            s.drain()
            s.activation(pq(7), pq(7), AF.Exp, scale=-0.5)
            s.drain()
            s_s.inc(s.nop())

        @b.tensor
        def _(t):
            t.wait_ge(s_v.h, s_v.n)
            t.matmul(psColB[:, :], onesrow_f[0:1, :],
                     constc[0:1, C_CSC:C_CSC + 1], start=True, stop=True)
            t.drain()
            s_t.inc(t.nop())

        @b.vector
        def _(v):
            v.wait_ge(s_t.h, s_t.n)
            v.wait_ge(s_s.h, s_s.n)
            v.tensor_copy(csc_col[:, :], psColB[:, :])
            v.tensor_copy(rcols[:, :], pq(7))
            v.tensor_tensor(pq(1), pq(1), pq(2), ALU.add)
            v.drain()
            v.tensor_tensor(pq(1), pq(1), pq(4), ALU.add)
            v.tensor_tensor(pq(2), pq(6), pq(7), ALU.mult)
            v.drain()
            v.tensor_scalar(pq(0), pq(6), csc_col[:, 0:1], None, op0=ALU.mult)
            v.drain()
            v.tensor_tensor(pq(1), pq(1), pq(0), ALU.subtract)
            v.drain()
            v.tensor_tensor(pq(1), pq(1), pq(7), ALU.mult)
            v.drain()
            s_v.inc(v.engine_nop())

        @b.tensor
        def _(t):
            t.wait_ge(s_v.h, s_v.n)
            t.matmul(psBLK[:, :], halfon[:, :], pq(1), start=True, stop=True)
            t.matmul(psSMR[:, :], onec_f[:, :], pq(2), start=True, stop=True)
            t.drain()
            s_t.inc(t.nop())

        @b.vector
        def _(v):
            v.wait_ge(s_t.h, s_t.n)
            v.tensor_copy(blk2[:, :], psBLK[:, :])
            v.tensor_scalar(constc[0:1, 416:448], psSMR[:, :], -1.0, None,
                            op0=ALU.mult, op1=ALU.add,
                            accum_out=constc[0:1, C_SMR:C_SMR + 1])
            v.drain()
            v.tensor_copy(rows_bf[0:1, 384:385], constc[0:1, C_SMR:C_SMR + 1])
            v.drain()
            s_v.inc(v.engine_nop())

        @b.gpsimd
        def _(g):
            g.wait_ge(s_v.h, s_v.n)
            for h2 in range(2):
                s_gat.inc(g.dma_start(
                    constc[0:1, C_BLK + 32 * h2:C_BLK + 32 * h2 + 32],
                    blk2[h2:h2 + 1, :]), 16)

    with nc.Block() as b:
        @b.vector
        def _(v):
            v.wait_ge(s_gat.h, s_gat.n)
            v.max(constc[0:1, C_T8V:C_T8V + 8], constc[0:1, C_BLK:C_BLK + 64])
            v.drain()
            v.max_index(idx8[:, :], constc[0:1, C_T8V:C_T8V + 8],
                        constc[0:1, C_BLK:C_BLK + 64])
            v.drain()
            v.tensor_copy(constc[0:1, C_BLK4:C_BLK4 + 4], idx8[0:1, 0:4])
            v.drain()
            v.tensor_scalar(constc[0:1, C_T8V:C_T8V + 4],
                            constc[0:1, C_BLK4:C_BLK4 + 4], 32.0, 4032.0,
                            op0=ALU.is_ge, op1=ALU.mult)
            v.drain()
            v.tensor_scalar_mul(constc[0:1, C_BLK4:C_BLK4 + 4],
                                constc[0:1, C_BLK4:C_BLK4 + 4], 128.0)
            v.drain()
            v.tensor_tensor(constc[0:1, C_BLK4:C_BLK4 + 4],
                            constc[0:1, C_BLK4:C_BLK4 + 4],
                            constc[0:1, C_T8V:C_T8V + 4], ALU.subtract)
            v.drain()
            v.tensor_reduce(constc[0:1, C_MINP:C_MINP + 1],
                            constc[0:1, C_BLK4:C_BLK4 + 4], AX.X, ALU.min)
            v.drain()
            for j in range(4):
                v.tensor_scalar(constc[0:1, C_OFF + 64 * j:C_OFF + 64 * j + 64],
                                constc[0:1, C_IOTA:C_IOTA + 64],
                                constc[0:1, C_BLK4 + j:C_BLK4 + j + 1], None,
                                op0=ALU.add)
                v.drain()
                s_v.inc(v.engine_nop())

    with (
        nc.psum_tensor("psCol2", [128, 8], F32) as psCol2,
        nc.Block() as b,
    ):
        @b.tensor
        def _(t):
            for cc in range(2):
                t.matmul(psCol2[:, cc:cc + 1],
                         constc[0:1, C_OFF + 128 * cc:C_OFF + 128 * cc + 128],
                         one1_f[:, :], start=(cc == 0), stop=True,
                         skip_group_check=True)
            t.drain()
            s_t.inc(t.nop())

        @b.vector
        def _(v):
            v.wait_ge(s_t.h, s_t.n)
            v.tensor_copy(pos_f[:, :], psCol2[:, 0:2])
            v.tensor_copy(idx_u32[:, :], psCol2[:, 0:2])

        @b.sync
        def _(sy):
            s_dw.inc(sy.dma_start(rows[0:1, 0:T], qiota_d.ap()), 16)

    with nc.Block() as b:
        @b.gpsimd
        def _(g):
            g.wait_ge(s_out.h, s_out.n)
            for cc in range(2):
                s_gat.inc(g.indirect_dma_start(
                    ab(A_LSEL + cc * R, R), None, lat_dram.ap(),
                    IndirectOffsetOnAxis(idx_u32[:, cc:cc + 1], 0)), 16)
                s_gat.inc(g.indirect_dma_start(
                    AFX[:, cc * 1024:(cc + 1) * 1024], None, xtm_d.ap(),
                    IndirectOffsetOnAxis(idx_u32[:, cc:cc + 1], 0)), 16)
                s_gat.inc(g.indirect_dma_start(
                    ab(A_COSS + cc * D, D), None, costm_d.ap(),
                    IndirectOffsetOnAxis(idx_u32[:, cc:cc + 1], 0)), 16)
                s_gat.inc(g.indirect_dma_start(
                    ab(A_SINS + cc * D, D), None, sintm_d.ap(),
                    IndirectOffsetOnAxis(idx_u32[:, cc:cc + 1], 0)), 16)

    with nc.Block() as b:
        @b.vector
        def _(v):
            v.wait_ge(s_gat.h, s_gat.n)
            for cc in range(2):
                xs = AFX[:, cc * 1024:(cc + 1) * 1024]
                v.reduce_sum(msel_c[:, cc:cc + 1], xs, axis=AX.X)
                v.drain()
                v.tensor_tensor(xs, xs, xs, ALU.mult)
                v.drain()
                v.reduce_sum(rsel_c[:, cc:cc + 1], xs, axis=AX.X)
            v.drain()
            v.tensor_scalar_mul(msel_c[:, :], msel_c[:, :], 1.0 / 1024.0)
            v.tensor_scalar_mul(rsel_c[:, :], rsel_c[:, :], 1.0 / 1024.0)
            v.drain()
            v.tensor_tensor(AFX[:, 0:2], msel_c[:, :], msel_c[:, :], ALU.mult)
            v.drain()
            v.tensor_tensor(rsel_c[:, :], rsel_c[:, :], AFX[:, 0:2],
                            ALU.subtract)
            v.drain()
            s_v.inc(v.engine_nop())

        @b.scalar
        def _(s):
            s.wait_ge(s_v.h, s_v.n)
            s.activation(rsel_c[:, :], rsel_c[:, :], AF.Ln, bias=EPS)
            s.drain()
            s.activation(rsel_c[:, :], rsel_c[:, :], AF.Exp, scale=-0.5)
            s.drain()
            s_s.inc(s.nop())

    with (
        nc.psum_tensor("psTa", [128, 128], BF16) as psTa,
        nc.psum_tensor("psTb", [128, 128], BF16) as psTb,
        nc.psum_tensor("psTf", [1, 128], F32) as psTf,
        nc.psum_tensor("psTf2", [1, 128], F32) as psTf2,
        nc.psum_tensor("psMR", [128, 1], F32) as psMR,
        nc.Block() as b,
    ):
        @b.gpsimd
        def _(g):
            for hf2 in range(2):
                if hf2 == 1:
                    g.wait_ge(s_pw.h, 16)
                s_pw.inc(g.dma_start(
                    wob_d.ap().rearrange("p (dt o) -> p dt o", o=D)[
                        :, hf2 * 4:(hf2 + 1) * 4, :],
                    woutT_d.ap().rearrange("(dt p) o -> p dt o", p=128)[
                        :, hf2 * 4:(hf2 + 1) * 4, :]), 16)
            issue_precast(g, 20)

        @b.tensor
        def _(t):
            t.transpose(psTa[:, :], ab(A_LSEL, R), ident[:, :])
            t.transpose(psTb[:, :], ab(A_LSEL + R, R), ident[:, :])
            t.transpose(psTf[:, :], msel_c[:, 0:1], identf[:, :])
            t.transpose(psTf2[:, :], msel_c[:, 1:2], identf[:, :])
            for tt in range(32):
                t.matmul(psMR[:, :], ab(A_LAT + tt * R, R), rcols[:, tt:tt + 1],
                         start=(tt == 0), stop=False)
            t.matmul(psMR[:, :], rows_bf[0:1, 256:256 + R],
                     rows_bf[0:1, 384:385],
                     start=False, stop=True)
            t.drain()
            s_t.inc(t.nop())

        @b.vector
        def _(v):
            v.wait_ge(s_t.h, s_t.n)
            v.tensor_copy(ab(A_LSELF, R), psTa[:, :])
            v.tensor_copy(ab(A_LSELF + R, R), psTb[:, :])
            v.tensor_scalar_mul(rows_bf[0:1, 0:128], psTf[0:1, :], -1.0)
            v.tensor_scalar_mul(rows_bf[0:1, 128:256], psTf2[0:1, :], -1.0)
            v.tensor_scalar_mul(mlat_c[:, :], psMR[:, :], 1.0 / S)
            v.drain()
            s_v.inc(v.engine_nop())

    with (
        nc.psum_tensor("psRowB", [1, 1024], F32) as psRowB,
        nc.Block() as b,
    ):
        w1k_t0, w1k_v0 = s_t.n, s_v.n

        @b.tensor
        def _(t):
            for half in range(2):
                if half >= 1:
                    t.wait_ge(s_v.h, w1k_v0 + 1)
                for ch in range(2):
                    t.matmul(psRowB[:, 512 * ch:512 * (ch + 1)], w1d_c[:, :],
                             ab(A_WKU + half * 1024 + 512 * ch, 512),
                             start=True, stop=True, skip_group_check=True)
                t.drain()
                s_t.inc(t.nop())

        @b.vector
        def _(v):
            for half in range(2):
                v.wait_ge(s_t.h, w1k_t0 + half + 1)
                v.tensor_copy(rows_bf[0:1, 512 + half * 1024:
                                      512 + (half + 1) * 1024], psRowB[:, :])
                v.drain()
                s_v.inc(v.engine_nop())

    with (
        nc.psum_tensor("psRowC", [1, 1024], F32) as psRowC,
        nc.Block() as b,
    ):
        @b.tensor
        def _(t):
            for ch in range(2):
                t.matmul(psRowC[:, 512 * ch:512 * (ch + 1)], mlat_c[:, :],
                         ab(A_WKU + D + 512 * ch, 512),
                         start=True, stop=True, skip_group_check=True)
            t.drain()
            s_t.inc(t.nop())

        @b.vector
        def _(v):
            v.wait_ge(s_t.h, s_t.n)
            v.memset(rows_bf[0:1, 2560:3600], 1.0)
            v.drain()
            v.tensor_copy(
                rows_bf[0:1, 2560:3600].rearrange(
                    "p (h e) -> p h e", e=65)[:, :, 0:64],
                psRowC[:, :].rearrange("p (h e) -> p h e", e=64))

    with (
        nc.psum_tensor("psW", [128, 512], F32) as psW,
        nc.psum_tensor("psW2", [128, 512], F32) as psW2,
        nc.Block() as b,
    ):
        psWp = [psW, psW2]

        @b.tensor
        def _(t):
            for mt in range(2):
                for ch in range(4):
                    i = mt * 4 + ch
                    if i >= 2:
                        t.wait_ge(s_v.h, s_v.n + i - 1)
                    p = psWp[i % 2]
                    t.matmul(p[:, :], ab(A_LSELF + mt * R, R),
                             ab(A_WKU + ch * 512, 512),
                             start=True, stop=False)
                    t.matmul(p[:, :], rows_bf[0:1, mt * 128:(mt + 1) * 128],
                             rows_bf[0:1, 512 + ch * 512:512 + (ch + 1) * 512],
                             start=False, stop=True)
                    t.drain()
                    s_t.inc(t.nop())

        @b.vector
        def _(v):
            tbase = s_t.n - 8
            vbase = s_v.n
            for mt in range(2):
                for ch in range(4):
                    i = mt * 4 + ch
                    v.wait_ge(s_t.h, tbase + i + 1)
                    p = psWp[i % 2]
                    if ch < 2:
                        v.tensor_scalar(ab(A_KSEL + mt * D + ch * 512, 512),
                                        p[:, :], rsel_c[:, mt:mt + 1], None,
                                        op0=ALU.mult)
                    else:
                        base = A_VBUF + mt * 1088 + (ch - 2) * 8 * 65
                        v.tensor_scalar(
                            ABF[:, base:base + 520].rearrange(
                                "p (h e) -> p h e", e=65)[:, :, 0:64],
                            p[:, :].rearrange("p (h e) -> p h e", e=64),
                            rsel_c[:, mt:mt + 1], None, op0=ALU.mult)
                    v.drain()
                    s_v.inc(v.engine_nop())

    with nc.Block() as b:
        @b.vector
        def _(v):
            for mt in range(2):
                for blk in range(8):
                    par = blk ^ 1
                    v.tensor_tensor(ab(A_KTMP, 128),
                                    ab(A_KSEL + mt * D + blk * 128, 128),
                                    ab(A_COSS + mt * D + blk * 128, 128),
                                    ALU.mult)
                    v.tensor_tensor(ab(A_KTMP + 128, 128),
                                    ab(A_KSEL + mt * D + par * 128, 128),
                                    ab(A_SINS + mt * D + blk * 128, 128),
                                    ALU.mult)
                    v.drain()
                    v.tensor_tensor(ab(A_KROT + mt * D + blk * 128, 128),
                                    ab(A_KTMP, 128), ab(A_KTMP + 128, 128),
                                    ALU.add)
                    v.drain()

    with (
        nc.psum_tensor("psTa2", [128, 128], BF16) as psTa2,
        nc.psum_tensor("psTb2", [128, 128], BF16) as psTb2,
        nc.Block() as b,
    ):
        psTp = [psTa2, psTb2]

        @b.tensor
        def _(t):
            for i in range(16):
                mt, dt = i % 2, i // 2
                if i >= 2:
                    t.wait_ge(s_v.h, s_v.n + i - 1)
                t.transpose(psTp[i % 2][:, :],
                            ab(A_KROT + mt * D + dt * 128, 128), ident[:, :])
                t.drain()
                s_t.inc(t.nop())

        @b.vector
        def _(v):
            tbase = s_t.n - 16
            for i in range(16):
                mt, dt = i % 2, i // 2
                v.wait_ge(s_t.h, tbase + i + 1)
                v.tensor_copy(ab(A_KFM + dt * 256 + mt * 128, 128),
                              psTp[i % 2][:, :])
                v.drain()
                s_v.inc(v.engine_nop())

    with (
        nc.psum_tensor("psQB", [128, 512], F32) as psQB,
        nc.psum_tensor("psQB2", [128, 512], F32) as psQB2,
        nc.Block() as b,
    ):
        psQp = [psQB, psQB2]

        @b.tensor
        def _(t):
            t.wait_ge(s_dw.h, s_dw.n)
            for qc in range(4):
                if qc >= 2:
                    t.wait_ge(s_v.h, s_v.n + 2 * (qc - 1) + 1)
                t.matmul(psQp[qc % 2][:, :], onesrow_f[0:1, :],
                         rows[0:1, qc * 512:(qc + 1) * 512],
                         start=True, stop=True)
                t.drain()
                s_t.inc(t.nop())

        @b.vector
        def _(v):
            tbase = s_t.n - 4
            v.wait_ge(s_dw.h, s_dw.n)
            v.tensor_scalar(rows[96:97, 0:T], rows[0:1, 0:T],
                            constc[0:1, C_MINP:C_MINP + 1], -1.0,
                            op0=ALU.subtract, op1=ALU.mult)
            v.drain()
            v.tensor_scalar(rows[96:97, 0:T], rows[96:97, 0:T], 0.0, 1.0,
                            op0=ALU.max, op1=ALU.min)
            v.drain()
            v.tensor_copy(rows_bf[0:1, 0:T], rows[96:97, 0:T])
            v.drain()
            s_v.inc(v.engine_nop())
            for qc in range(4):
                v.wait_ge(s_t.h, tbase + qc + 1)
                for mt in range(2):
                    v.tensor_scalar(ab(A_MASK + mt * T + qc * 512, 512),
                                    psQp[qc % 2][:, :], pos_f[:, mt:mt + 1],
                                    None, op0=ALU.is_ge)
                    v.drain()
                    s_v.inc(v.engine_nop())

    with nc.Block() as b:
        @b.gpsimd
        def _(g):
            s_ff.inc(g.dma_start(
                ab(A_WQ, 8192).rearrange("p (dt o) -> p dt o", o=D),
                wqT_d.ap().rearrange("(dt p) o -> p dt o", p=128)), 16)

    with (
        nc.psum_tensor("psW3", [128, 512], F32) as psW3,
        nc.psum_tensor("psW4", [128, 512], F32) as psW4,
        nc.psum_tensor("psQ1", [1, 512], F32) as psQ1,
        nc.psum_tensor("psQ2", [1, 512], F32) as psQ2,
        nc.Block() as b,
    ):
        psWp = [psW3, psW4]

        psQw = [psQ1, psQ2]
        w1q_t0 = s_t.n

        @b.tensor
        def _(t):
            t.wait_ge(s_ff.h, s_ff.n)
            for ch in range(2):
                for d in range(8):
                    t.matmul(psQw[ch][:, :], onec_bf[:, :],
                             ab(A_WQ + d * D + ch * 512, 512),
                             start=(d == 0), stop=(d == 7),
                             skip_group_check=True)
                t.drain()
                s_t.inc(t.nop())

        @b.vector
        def _(v):
            for ch in range(2):
                v.wait_ge(s_t.h, w1q_t0 + 1 + ch)
                v.tensor_copy(rows_bf[0:1, 3600 + ch * 512:3600 + (ch + 1) * 512],
                              psQw[ch][:, :])
                v.drain()
                s_v.inc(v.engine_nop())

    with (
        nc.psum_tensor("psW5", [128, 512], F32) as psW5,
        nc.psum_tensor("psW6", [128, 512], F32) as psW6,
        nc.psum_tensor("psQ3", [1, 512], F32) as psQ3,
        nc.psum_tensor("psQ4", [1, 512], F32) as psQ4,
        nc.psum_tensor("psQ5", [1, 512], F32) as psQ5,
        nc.psum_tensor("psQ6", [1, 512], F32) as psQ6,
        nc.Block() as b,
    ):
        psWp = [psW5, psW6]
        psQ3p = [psQ3, psQ5]
        psQ4p = [psQ4, psQ6]
        t_ctr = [s_t.n]
        v_ctr = [s_v.n]
        s_ctr = [s_s.n]
        h_ctr = [s_dh.n]

        @b.gpsimd
        def _(g):
            for c in range(4):
                if c >= 1:
                    g.wait_ge(s_t.h, t_ctr[0] + c * 10)
                s_dh.inc(g.dma_start(
                    ab(A_XQH, 4096).rearrange("p (dt t) -> p dt t", t=512),
                    xTq_d.ap()[:, c * 512:(c + 1) * 512].rearrange(
                        "(dt p) t -> p dt t", p=128)), 16)

        @b.vector
        def _(v):
            for c in range(4):
                v.wait_ge(s_dh.h, h_ctr[0] + 16 * (c + 1))
                v.tensor_tensor(ab(A_XSQ, 4096), ab(A_XQH, 4096),
                                ab(A_XQH, 4096), ALU.mult)
                v.drain()
                s_v.inc(v.engine_nop())
                v.wait_ge(s_t.h, t_ctr[0] + c * 10 + 2)
                cs = slice(c * 512, (c + 1) * 512)
                mq = rows[0:1, 2048 + c * 512:2048 + (c + 1) * 512]
                tmp = constc[0:1, 416:928]
                v.tensor_copy(mq, psQ3p[c % 2][:, :])
                v.tensor_copy(rows[0:1, cs], psQ4p[c % 2][:, :])
                v.drain()
                v.tensor_scalar_mul(mq, mq, 1.0 / 1024.0)
                v.drain()
                v.tensor_tensor(tmp, mq, mq, ALU.mult)
                v.tensor_scalar_mul(rows[0:1, cs], rows[0:1, cs],
                                    1.0 / 1024.0)
                v.drain()
                v.tensor_tensor(rows[0:1, cs], rows[0:1, cs], tmp,
                                ALU.subtract)
                v.tensor_scalar_mul(rows_bf[0:1, 2048:2560], mq, -1.0)
                v.drain()
                s_v.inc(v.engine_nop())
                for mt in range(8):
                    v.wait_ge(s_t.h, t_ctr[0] + c * 10 + 3 + mt)
                    v.tensor_copy(ab(A_QBUF + mt * T + c * 512, 512),
                                  psWp[mt % 2][:, :])
                    v.drain()
                    s_v.inc(v.engine_nop())

        @b.scalar
        def _(s):
            for c in range(4):
                cs = slice(c * 512, (c + 1) * 512)
                s.wait_ge(s_v.h, v_ctr[0] + c * 10 + 2)
                s.activation(rows[0:1, cs], rows[0:1, cs], AF.Ln, bias=EPS)
                s.drain()
                s.activation(rows[0:1, cs], rows[0:1, cs], AF.Exp, scale=-0.5)
                s.drain()
                s_s.inc(s.nop())

        @b.tensor
        def _(t):
            for c in range(4):
                t.wait_ge(s_dh.h, h_ctr[0] + 16 * (c + 1))
                t.wait_ge(s_v.h, v_ctr[0] + c * 10 + 1)
                for d in range(8):
                    mm3 = t.matmul(psQ3p[c % 2][:, :], onec_bf[:, :],
                                   ab(A_XQH + d * 512, 512),
                                   start=(d == 0), stop=(d == 7))
                s_t.inc(mm3)
                for d in range(8):
                    mm4 = t.matmul(psQ4p[c % 2][:, :], onec_bf[:, :],
                                   ab(A_XSQ + d * 512, 512),
                                   start=(d == 0), stop=(d == 7))
                s_t.inc(mm4)
                for mt in range(8):
                    p = psWp[mt % 2]
                    if mt >= 2:
                        t.wait_ge(s_v.h, v_ctr[0] + c * 10 + 2 + (mt - 1))
                    for d in range(8):
                        t.matmul(p[:, :],
                                 ab(A_WQ + d * D + mt * 128, 128),
                                 ab(A_XQH + d * 512, 512),
                                 start=(d == 0), stop=False)
                    if mt == 0:
                        t.wait_ge(s_v.h, v_ctr[0] + c * 10 + 2)
                    mmq = t.matmul(
                        p[:, :],
                        rows_bf[0:1, 3600 + mt * 128:3600 + (mt + 1) * 128],
                        rows_bf[0:1, 2048:2560],
                        start=False, stop=True)
                    s_t.inc(mmq)

    with (
        nc.psum_tensor("psW7", [128, 512], F32) as psW7,
        nc.psum_tensor("psW8", [128, 512], F32) as psW8,
        nc.Block() as b,
    ):
        psWp = [psW7, psW8]
        t0 = [s_t.n]

        @b.tensor
        def _(t):
            for qc in range(4):
                if qc >= 2:
                    t.wait_ge(s_v.h, s_v.n + 3 * (qc - 1))
                t.matmul(psWp[qc % 2][:, :], onesrow_f[0:1, :],
                         rows[0:1, qc * 512:(qc + 1) * 512],
                         start=True, stop=True)
                t.drain()
                s_t.inc(t.nop())

        @b.vector
        def _(v):
            tb = s_t.n - 4
            for qc in range(4):
                v.wait_ge(s_t.h, tb + qc + 1)
                sl = slice(qc * 512, (qc + 1) * 512)
                v.tensor_tensor(ab(A_TABC, 2048)[:, sl], ab(A_FMTAB, 2048)[:, sl],
                                psWp[qc % 2][:, :], ALU.mult)
                v.tensor_tensor(ab(A_TABSP, 2048)[:, sl],
                                ab(A_FMTAB + 2048, 2048)[:, sl],
                                psWp[qc % 2][:, :], ALU.mult)
                v.tensor_tensor(ab(A_TABSN, 2048)[:, sl],
                                ab(A_FMTAB + 4096, 2048)[:, sl],
                                psWp[qc % 2][:, :], ALU.mult)
                v.drain()
                s_v.inc(v.engine_nop())
                v.drain()
                s_v.inc(v.engine_nop())
                v.drain()
                s_v.inc(v.engine_nop())

    with nc.Block() as b:
        @b.vector
        def _(v):
            for pp in range(4):
                q0 = ab(A_QBUF + (2 * pp) * T, T)
                q1 = ab(A_QBUF + (2 * pp + 1) * T, T)
                v.tensor_tensor(ab(A_QT1, T), q0, ab(A_TABC, T), ALU.mult)
                v.tensor_tensor(ab(A_QT2, T), q1, ab(A_TABSN, T), ALU.mult)
                v.tensor_tensor(ab(A_QT3, T), q1, ab(A_TABC, T), ALU.mult)
                v.tensor_tensor(ab(A_QT4, T), q0, ab(A_TABSP, T), ALU.mult)
                v.drain()
                v.tensor_tensor(q0, ab(A_QT1, T), ab(A_QT2, T), ALU.add)
                v.tensor_tensor(q1, ab(A_QT3, T), ab(A_QT4, T), ALU.add)
                v.drain()

    with (
        nc.psum_tensor("psS0", [128, 1024], F32) as psS0,
        nc.psum_tensor("psS1", [128, 1024], F32) as psS1,
        nc.psum_tensor("psPV0", [128, 1024], F32) as psPV0,
        nc.psum_tensor("psPV1", [128, 1024], F32) as psPV1,
        nc.Block() as b,
    ):
        psSp = [psS0, psS1]
        psPVp = [psPV0, psPV1]
        t0, v0, sc0 = s_t.n, s_v.n, s_s.n
        NST = 8

        @b.gpsimd
        def _(g):
            issue_precast(g, len(_pc_jobs))

        def tqk(qc, st, mt):
            return t0 + (qc * NST + st) * 3 + mt + 1

        def tpv(qc, st):
            return t0 + (qc * NST + st) * 3 + 3

        def sexp(qc, st, mt):
            return sc0 + (qc * NST + st) * 2 + mt + 1

        def vmsk(qc, st, mt):
            return v0 + (qc * NST + st) * 3 + mt + 1

        def vevac(qc, st):
            return v0 + (qc * NST + st) * 3 + 3

        @b.tensor
        def _(t):
            for qc in range(4):
                for st in range(NST):
                    gi = qc * NST + st
                    for mt in range(2):
                        par = (gi * 2 + mt) % 2
                        if gi * 2 + mt >= 2:
                            t.wait_ge(s_s.h, sc0 + gi * 2 + mt - 1)
                        for hh in range(2):
                            h = 2 * st + hh
                            j = h % 4
                            tp = h // 4
                            t.matmul(psSp[par][:, hh * 512:(hh + 1) * 512],
                                     ab(A_KFM + (2 * tp) * 256 + mt * 128, 128,
                                        32 * j, 32 * j + 32),
                                     ab(A_QBUF + (2 * tp) * T + qc * 512, 512,
                                        32 * j, 32 * j + 32),
                                     start=True, stop=False,
                                     tile_position=(32 * j, 0))
                            mmq = t.matmul(
                                psSp[par][:, hh * 512:(hh + 1) * 512],
                                ab(A_KFM + (2 * tp + 1) * 256 + mt * 128,
                                   128, 32 * j, 32 * j + 32),
                                ab(A_QBUF + (2 * tp + 1) * T + qc * 512,
                                   512, 32 * j, 32 * j + 32),
                                start=False, stop=True,
                                tile_position=(32 * j, 0))
                        s_t.inc(mmq)
                    pvpar = gi % 2
                    if gi >= 2:
                        t.wait_ge(s_v.h, vevac(qc, st - 2) if st >= 2 else
                                  vevac(qc - 1, st + NST - 2))
                    t.wait_ge(s_v.h, vmsk(qc, st, 1))
                    first_bank0 = True
                    first_bank1 = True
                    for hh in range(2):
                        h = 2 * st + hh
                        for qt in range(4):
                            col = (hh * 4 + qt) * 128
                            bank = col // 512
                            fb = first_bank0 if bank == 0 else first_bank1
                            for mt in range(2):
                                t.matmul(
                                    psPVp[pvpar][:, col:col + 65],
                                    ab(A_PBUF + ((gi % 2) * 2 + mt) * 1024 +
                                       hh * 512 + qt * 128, 128),
                                    ab(A_VBUF + mt * 1088 + h * 65, 65),
                                    start=(fb and mt == 0), stop=False,
                                    skip_group_check=True)
                                if bank == 0:
                                    first_bank0 = False
                                else:
                                    first_bank1 = False
                                fb = False
                            mmu = t.matmul(psPVp[pvpar][:, col:col + 65],
                                           rows_bf[0:1,
                                                   qc * 512 + qt * 128:
                                                   qc * 512 + qt * 128 + 128],
                                           rows_bf[0:1, 2560 + h * 65:
                                                   2560 + h * 65 + 65],
                                           start=False, stop=True,
                                           skip_group_check=True)
                    s_t.inc(mmu)

        @b.scalar
        def _(s):
            for qc in range(4):
                for st in range(NST):
                    gi = qc * NST + st
                    for mt in range(2):
                        par = (gi * 2 + mt) % 2
                        k = gi * 2 + mt
                        if k >= 2:
                            s.wait_ge(s_v.h, v0 + (gi - 1) * 3 + mt + 1)
                        s.wait_ge(s_t.h, tqk(qc, st, mt))
                        s_s.inc(s.activation(
                            ab(A_PTMP + par * 1024, 1024), psSp[par][:, :],
                            AF.Exp, scale=0.125))

        @b.vector
        def _(v):
            for qc in range(4):
                for st in range(NST):
                    gi = qc * NST + st
                    for mt in range(2):
                        par = (gi * 2 + mt) % 2
                        if gi >= 2:
                            v.wait_ge(s_t.h, tpv(*divmod(gi - 2, NST)))
                        v.wait_ge(s_s.h, sexp(qc, st, mt))
                        for hh in range(2):
                            tt = v.tensor_tensor(
                                ab(A_PBUF + ((gi % 2) * 2 + mt) * 1024 +
                                   hh * 512, 512),
                                ab(A_PTMP + par * 1024 + hh * 512, 512),
                                ab(A_MASK + mt * T + qc * 512, 512), ALU.mult)
                        s_v.inc(tt)
                    pvpar = gi % 2
                    v.wait_ge(s_t.h, tpv(qc, st))
                    v.reciprocal(recb[:, 0:8],
                                 psPVp[pvpar][:, :].rearrange(
                                     "p (a c) -> p a c", c=128)[:, :, 64:65])
                    v.drain()
                    for hh in range(2):
                        h = 2 * st + hh
                        for qt in range(4):
                            col = (hh * 4 + qt) * 128
                            ts = v.tensor_scalar(
                                ab(atm_off(qc * 4 + qt) + h * 64, 64),
                                psPVp[pvpar][:, col:col + 64],
                                recb[:, hh * 4 + qt:hh * 4 + qt + 1], None,
                                op0=ALU.mult)
                    s_v.inc(ts)

    with nc.Block() as b:
        @b.sync
        def _(sy):
            sy.wait_ge(s_pw.h, 32)
            s_dw.inc(sy.dma_start(ab(A_WOUT, 8192), wob_d.ap()), 16)

    with (
        nc.psum_tensor("psTc", [128, 512], BF16) as psTc,
        nc.psum_tensor("psTd", [128, 512], BF16) as psTd,
        nc.Block() as b,
    ):
        psTp = [psTc, psTd]
        t0, v0 = s_t.n, s_v.n
        afm_lo = ABF[:, 24704:24704 + 8192].rearrange("p (d c) -> p d c",
                                                      c=2048)
        afm_m = ABF[:, 24704 + 8192:24704 + 12288].rearrange(
            "p (d c) -> p d c", c=2048)
        afm_hi = ABF[:, 12288:16384].rearrange("p (d c) -> p d c", c=2048)

        @b.tensor
        def _(t):
            for g in range(32):
                qt, dtg = g // 2, g % 2
                par = g % 2
                if g >= 2:
                    t.wait_ge(s_v.h, v0 + g - 1)
                for k in range(4):
                    dt = dtg * 4 + k
                    tr = t.transpose(psTp[par][:, k * 128:(k + 1) * 128],
                                     ab(atm_off(qt) + dt * 128, 128),
                                     ident[:, :])
                s_t.inc(tr)

        @b.vector
        def _(v):
            for g in range(32):
                qt, dtg = g // 2, g % 2
                par = g % 2
                v.wait_ge(s_t.h, t0 + g + 1)
                src = psTp[par][:, :].rearrange("p (d c) -> p d c", c=128)
                if dtg == 0:
                    v.tensor_copy(afm_lo[:, :, qt * 128:qt * 128 + 128],
                                  src[:, 0:4, :])
                else:
                    v.tensor_copy(afm_m[:, :, qt * 128:qt * 128 + 128],
                                  src[:, 0:2, :])
                    v.tensor_copy(afm_hi[:, :, qt * 128:qt * 128 + 128],
                                  src[:, 2:4, :])
                v.drain()
                s_v.inc(v.engine_nop())

    def ln_rows(b, pm, cm, pr, cr, w, ln_scale=1.0):
        @b.vector
        def _(v):
            v.tensor_scalar_mul(rows[0:1, cm:cm + w],
                                rows[0:1, cm:cm + w], 1.0 / 1024.0)
            v.tensor_scalar_mul(rows[0:1, cr:cr + w],
                                rows[0:1, cr:cr + w], 1.0 / 1024.0)
            v.drain()
            for h0 in range(0, w, 1024):
                hw = min(1024, w - h0)
                v.tensor_tensor(AFX[0:1, 0:hw],
                                rows[0:1, cm + h0:cm + h0 + hw],
                                rows[0:1, cm + h0:cm + h0 + hw], ALU.mult)
                v.drain()
                v.tensor_tensor(rows[0:1, cr + h0:cr + h0 + hw],
                                rows[0:1, cr + h0:cr + h0 + hw],
                                AFX[0:1, 0:hw], ALU.subtract)
                v.drain()
                v.drain()
                s_v.inc(v.engine_nop())

        @b.scalar
        def _(s):
            s.wait_ge(s_v.h, s_v.n)
            s.activation(rows[0:1, cr:cr + w], rows[0:1, cr:cr + w],
                         AF.Ln, bias=EPS, scale=ln_scale * ln_scale)
            s.drain()
            s.activation(rows[0:1, cr:cr + w], rows[0:1, cr:cr + w],
                         AF.Exp, scale=-0.5)
            if ln_scale != 1.0:
                s.drain()
                s.mul(rows[0:1, cr:cr + w], rows[0:1, cr:cr + w], ln_scale)
            s.drain()
            s_s.inc(s.nop())

    with (
        nc.psum_tensor("psW9", [128, 512], F32) as psW9,
        nc.psum_tensor("psWa", [128, 512], F32) as psWa,
        nc.psum_tensor("psO1", [1, 512], F32) as psO1,
        nc.psum_tensor("psO3", [1, 512], F32) as psO3,
        nc.psum_tensor("psBm", [128, 512], F32) as psBm,
        nc.psum_tensor("psBr", [128, 512], F32) as psBr,
        nc.Block() as b,
    ):
        psWp = [psW9, psWa]
        t0, v0, sc0, g0 = s_t.n, s_v.n, s_s.n, s_g.n
        bcm, bcr = AFX[:, 1024:1536], AFX[:, 1536:2048]
        base_t = [0, 9, 19, 30]
        t_out = lambda qc, mt: base_t[qc] + mt + 1
        t_psO1 = [9, 18, 28, 39]
        t_psO3 = [19, 29, 40, 43]
        t_bc = [30, 41, 42, 44]
        vbase = [0, 5, 11, 19]
        v_evac = lambda qc, mt: vbase[qc] + mt + 1
        v_psO1c = [5, 10, 16, 24]
        v_pre = [11, 17, 25, 28]
        v_bcc = [18, 26, 29, 31]
        sbase = [0, 12, 25, 38]
        s_evac = lambda qc, mt: sbase[qc] + (mt - 4) + 1
        sqbase = [4, 16, 29, 42]
        s_sq = lambda qc, dt: sqbase[qc] + dt + 1
        s_fin = [25, 38, 51, 52]

        @b.tensor
        def _(t):
            def psO3_mms(qc):
                t.wait_ge(s_s.h, sc0 + s_sq(qc, 7))
                if qc >= 1:
                    t.wait_ge(s_v.h, v0 + v_pre[qc - 1])
                for dt in range(8):
                    mm = t.matmul(psO3[:, :], onec_bf[:, :],
                                  rows_bf[:, dt * 512:(dt + 1) * 512],
                                  start=(dt == 0), stop=(dt == 7))
                s_t.inc(mm)

            def bc_mms(qc):
                t.wait_ge(s_s.h, sc0 + s_fin[qc])
                if qc >= 1:
                    t.wait_ge(s_v.h, v0 + v_bcc[qc - 1])
                t.matmul(psBm[:, :], onesrow_f[0:1, :],
                         rows[0:1, qc * 512:(qc + 1) * 512],
                         start=True, stop=True)
                mm = t.matmul(psBr[:, :], onesrow_f[0:1, :],
                              rows[0:1, 2048 + qc * 512:2048 + (qc + 1) * 512],
                              start=True, stop=True)
                s_t.inc(mm)

            t.wait_ge(s_dw.h, s_dw.n)
            for qc in range(4):
                for mt in range(8):
                    if not (qc == 0 and mt < 2):
                        if mt >= 2:
                            tgt = mt - 2
                            if tgt <= 3:
                                t.wait_ge(s_v.h, v0 + v_evac(qc, tgt))
                            else:
                                t.wait_ge(s_s.h, sc0 + s_evac(qc, tgt))
                        else:
                            t.wait_ge(s_s.h,
                                      sc0 + s_evac(qc - 1, mt + 6))
                    for d in range(8):
                        mm = t.matmul(psWp[mt % 2][:, :],
                                      ab(A_WOUT + d * D + mt * 128, 128),
                                      ab(afm_off(d) + qc * 512, 512),
                                      start=(d == 0), stop=(d == 7))
                    s_t.inc(mm)
                t.wait_ge(s_v.h, v0 + v_evac(qc, 3))
                t.wait_ge(s_s.h, sc0 + s_evac(qc, 7))
                if qc >= 1:
                    t.wait_ge(s_v.h, v0 + v_psO1c[qc - 1])
                for dt in range(8):
                    mm = t.matmul(psO1[:, :], onec_bf[:, :],
                                  ab(obuf_off(dt) + qc * 512, 512),
                                  start=(dt == 0), stop=(dt == 7))
                s_t.inc(mm)
                if qc >= 1:
                    psO3_mms(qc - 1)
                if qc >= 2:
                    bc_mms(qc - 2)
                if qc == 3:
                    bc_mms(2)
            psO3_mms(3)
            bc_mms(3)

        @b.scalar
        def _(s):
            for qc in range(4):
                for mt in range(4, 8):
                    s.wait_ge(s_t.h, t0 + t_out(qc, mt))
                    s_s.inc(s.activation(ab(obuf_off(mt) + qc * 512, 512),
                                         psWp[mt % 2][:, :], AF.Copy))
                for dt in range(8):
                    if dt <= 3:
                        s.wait_ge(s_v.h, v0 + v_evac(qc, dt))
                    if dt == 0 and qc >= 1:
                        s.wait_ge(s_t.h, t0 + t_psO3[qc - 1])
                    s_s.inc(s.activation(
                        rows_bf[:, dt * 512:(dt + 1) * 512],
                        ab(obuf_off(dt) + qc * 512, 512), AF.Square))
                if qc >= 1:
                    qq = qc - 1
                    s.wait_ge(s_v.h, v0 + v_pre[qq])
                    sl = slice(2048 + qq * 512, 2048 + (qq + 1) * 512)
                    s.activation(rows[0:1, sl], rows[0:1, sl], AF.Ln,
                                 bias=EPS)
                    s.drain()
                    s.activation(rows[0:1, sl], rows[0:1, sl], AF.Exp,
                                 scale=-0.5)
                    s.drain()
                    s_s.inc(s.nop())
            qq = 3
            s.wait_ge(s_v.h, v0 + v_pre[qq])
            sl = slice(2048 + qq * 512, 2048 + (qq + 1) * 512)
            s.activation(rows[0:1, sl], rows[0:1, sl], AF.Ln, bias=EPS)
            s.drain()
            s.activation(rows[0:1, sl], rows[0:1, sl], AF.Exp, scale=-0.5)
            s.drain()
            s_s.inc(s.nop())

        def apply_tiles(eng, qc, dts, tmpa):
            for dt in dts:
                eng.tensor_tensor(tmpa, ab(obuf_off(dt) + qc * 512, 512),
                                  bcm, ALU.subtract)
                eng.drain()
                eng.tensor_tensor(tmpa, tmpa, bcr, ALU.mult)
                eng.drain()
                xs = xq[:, dt * T + qc * 512:dt * T + qc * 512 + 512]
                eng.tensor_tensor(xs, xs, tmpa, ALU.add)
                eng.drain()

        @b.vector
        def _(v):
            def pre_ops(qc):
                v.wait_ge(s_t.h, t0 + t_psO3[qc])
                ms = rows[0:1, qc * 512:(qc + 1) * 512]
                ss = rows[0:1, 2048 + qc * 512:2048 + (qc + 1) * 512]
                tmp = constc[0:1, 416:928]
                v.tensor_copy(ss, psO3[:, :])
                v.drain()
                v.tensor_scalar_mul(ms, ms, 1.0 / 1024.0)
                v.tensor_scalar_mul(ss, ss, 1.0 / 1024.0)
                v.drain()
                v.tensor_tensor(tmp, ms, ms, ALU.mult)
                v.drain()
                v.tensor_tensor(ss, ss, tmp, ALU.subtract)
                v.drain()
                s_v.inc(v.engine_nop())

            def bcc_apply(qc):
                v.wait_ge(s_t.h, t0 + t_bc[qc])
                if qc >= 1:
                    v.wait_ge(s_g.h, g0 + qc)
                v.tensor_copy(bcm, psBm[:, :])
                v.tensor_copy(bcr, psBr[:, :])
                v.drain()
                s_v.inc(v.engine_nop())
                apply_tiles(v, qc, range(0, 5), AFX[:, 0:512])
                s_v.inc(v.engine_nop())

            v.wait_ge(s_y.h, 16)
            for qc in range(4):
                for mt in range(4):
                    v.wait_ge(s_t.h, t0 + t_out(qc, mt))
                    v.tensor_copy(ab(obuf_off(mt) + qc * 512, 512),
                                  psWp[mt % 2][:, :])
                    v.drain()
                    s_v.inc(v.engine_nop())
                v.wait_ge(s_t.h, t0 + t_psO1[qc])
                v.tensor_copy(rows[0:1, qc * 512:(qc + 1) * 512], psO1[:, :])
                v.drain()
                s_v.inc(v.engine_nop())
                if qc >= 1:
                    pre_ops(qc - 1)
                if qc >= 2:
                    bcc_apply(qc - 2)
            pre_ops(3)
            bcc_apply(2)
            bcc_apply(3)

        @b.gpsimd
        def _(g):
            g.wait_ge(s_y.h, 16)
            for qc in range(4):
                g.wait_ge(s_v.h, v0 + v_bcc[qc])
                apply_tiles(g, qc, range(5, 8), AFX[:, 512:1024])
                s_g.inc(g.engine_nop())

    X2L_B = [A_X2L, A_X2L]
    DN_B = [A_DN, A_X2L]
    for tc in range(2):
        TBASE = tc * 1024
        CM3 = tc * 2048
        with (
            nc.psum_tensor(f"psF1_{tc}", [1, 512], F32) as psF1,
            nc.psum_tensor(f"psF2_{tc}", [1, 512], F32) as psF2,
            nc.Block() as b,
        ):
            t0, v0, g0 = s_t.n, s_v.n, s_g.n

            def ln3_pre(eng, lo, hi, sem):
                for dt in range(lo, hi):
                    eng.tensor_copy(ab(A_DN + dt * 1024, 1024),
                                    xq[:, dt * T + TBASE:dt * T + TBASE + 1024])
                eng.drain()
                eng.tensor_tensor(ab(A_HIDN + lo * 1024, (hi - lo) * 1024),
                                  ab(A_DN + lo * 1024, (hi - lo) * 1024),
                                  ab(A_DN + lo * 1024, (hi - lo) * 1024),
                                  ALU.mult)
                eng.drain()
                sem.inc(eng.engine_nop())

            @b.vector
            def _(v):
                ln3_pre(v, 0, 4, s_v)

            @b.gpsimd
            def _(g):
                ln3_pre(g, 4, 8, s_g)

            @b.tensor
            def _(t):
                t.wait_ge(s_v.h, v0 + 1)
                t.wait_ge(s_g.h, g0 + 1)
                for half in range(2):
                    for dt in range(8):
                        t.matmul(psF1[:, :], onec_bf[:, :],
                                 ab(A_DN + dt * 1024 + half * 512, 512),
                                 start=(dt == 0), stop=(dt == 7))
                    for dt in range(8):
                        t.matmul(psF2[:, :], onec_bf[:, :],
                                 ab(A_HIDN + dt * 1024 + half * 512, 512),
                                 start=(dt == 0), stop=(dt == 7))
                    t.drain()
                    s_t.inc(t.nop())
                    t.wait_ge(s_v.h, v0 + 1 + half + 1)

            @b.vector
            def _(v):
                for half in range(2):
                    v.wait_ge(s_t.h, t0 + half + 1)
                    sl = slice(CM3 + half * 512, CM3 + half * 512 + 512)
                    v.tensor_copy(rows[0:1, sl], psF1[:, :])
                    v.tensor_copy(rows[0:1, 1024 + sl.start:1024 + sl.stop],
                                  psF2[:, :])
                    v.drain()
                    s_v.inc(v.engine_nop())

        with nc.Block() as b:
            ln_rows(b, 0, CM3, 0, CM3 + 1024, 1024)

        if tc == 0:
            with (
                nc.psum_tensor(f"psBm3_{tc}", [128, 512], F32) as psBm3,
                nc.psum_tensor(f"psBr3_{tc}", [128, 512], F32) as psBr3,
                nc.Block() as b,
            ):
                t0, v0, g0 = s_t.n, s_v.n, s_g.n
                bcm3, bcr3 = AFX[:, 1024:1536], AFX[:, 1536:2048]

                @b.tensor
                def _(t):
                    for half in range(2):
                        if half >= 1:
                            t.wait_ge(s_v.h, v0 + 1)
                        sl = slice(CM3 + half * 512, CM3 + half * 512 + 512)
                        t.matmul(psBm3[:, :], onesrow_f[0:1, :],
                                 rows[0:1, sl], start=True, stop=True)
                        t.matmul(psBr3[:, :], onesrow_f[0:1, :],
                                 rows[0:1, 1024 + sl.start:1024 + sl.stop],
                                 start=True, stop=True)
                        t.drain()
                        s_t.inc(t.nop())

                def x2l_tiles(eng, half, dts, tmpa):
                    for dt in dts:
                        xs = xq[:, dt * T + TBASE + half * 512:
                                dt * T + TBASE + half * 512 + 512]
                        eng.tensor_tensor(tmpa, xs, bcm3, ALU.subtract)
                        eng.drain()
                        eng.tensor_tensor(
                            ab(X2L_B[tc] + dt * 1024 + half * 512, 512),
                            tmpa, bcr3, ALU.mult)
                        eng.drain()

                @b.vector
                def _(v):
                    for half in range(2):
                        v.wait_ge(s_t.h, t0 + half + 1)
                        if half >= 1:
                            v.wait_ge(s_g.h, g0 + 4)
                        v.tensor_copy(bcm3, psBm3[:, :])
                        v.tensor_copy(bcr3, psBr3[:, :])
                        v.drain()
                        s_v.inc(v.engine_nop())
                        x2l_tiles(v, half, range(0, 5), AFX[:, 0:512])
                        s_v.inc(v.engine_nop(), 4)

                @b.gpsimd
                def _(g):
                    for half in range(2):
                        g.wait_ge(s_v.h, v0 + half * 5 + 1)
                        x2l_tiles(g, half, range(5, 8), AFX[:, 512:1024])
                        s_g.inc(g.engine_nop(), 4)

    for tc in range(2):
        TBASE = tc * 1024
        with (
            nc.psum_tensor(f"psG0_{tc}", [128, 1024], F32) as psG0,
            nc.psum_tensor(f"psG1_{tc}", [128, 1024], F32) as psG1,
            nc.psum_tensor(f"psU0_{tc}", [128, 1024], F32) as psU0,
            nc.psum_tensor(f"psU1_{tc}", [128, 1024], F32) as psU1,
            nc.Block() as b,
        ):
            psGp = [psG0, psG1]
            psUp = [psU0, psU1]
            t0, v0, sc0 = s_t.n, s_v.n, s_s.n
            swp = [s_wa, s_wb]
            w0 = [s_wa.n, s_wb.n]
            silp = [ab(A_SIL, 1024), AFX[:, 0:1024]]
            upp = [ab(A_UP, 1024), AFX[:, 1024:2048]]

            @b.sync
            def _(sy):
                if tc == 0:
                    sy.wait_ge(s_pc.h, PC_TOTAL)
                for ht in range(24):
                    par = ht % 2
                    if ht >= 2:
                        sy.wait_ge(s_t.h, t0 + ht - 1)
                    swp[par].inc(sy.dma_start(ab(A_GU + par * 2048, 2048),
                                              wgu_d.ap()[ht]), 16)

            @b.tensor
            def _(t):
                for ht in range(24):
                    par = ht % 2
                    t.wait_ge(swp[par].h, w0[par] + 16 * (ht // 2 + 1))
                    if ht >= 2:
                        t.wait_ge(s_s.h, sc0 + 2 * (ht - 1))
                    for half in range(2):
                        for d in range(8):
                            t.matmul(psGp[par][:, half * 512:half * 512 + 512],
                                     ab(A_GU + par * 2048 + d * 128, 128),
                                     ab(X2L_B[tc] + d * 1024 + half * 512,
                                        512),
                                     start=(d == 0), stop=(d == 7))
                    for half in range(2):
                        for d in range(8):
                            mm = t.matmul(
                                psUp[par][:, half * 512:half * 512 + 512],
                                ab(A_GU + par * 2048 + 1024 + d * 128, 128),
                                ab(X2L_B[tc] + d * 1024 + half * 512, 512),
                                start=(d == 0), stop=(d == 7))
                    s_t.inc(mm)

            @b.scalar
            def _(s):
                for ht in range(24):
                    par = ht % 2
                    s.wait_ge(s_t.h, t0 + ht + 1)
                    if ht >= 2:
                        s.wait_ge(s_v.h, v0 + ht - 1)
                    s_s.inc(s.activation(silp[par], psGp[par][:, :],
                                         AF.Silu, scale=CFF))
                    s_s.inc(s.activation(upp[par], psUp[par][:, :],
                                         AF.Copy, scale=CFF))

            @b.vector
            def _(v):
                for ht in range(24):
                    par = ht % 2
                    v.wait_ge(s_s.h, sc0 + 2 * ht + 2)
                    v.tensor_tensor(ab(A_HIDN + ht * 1024, 1024),
                                    silp[par], upp[par], ALU.mult)
                    v.drain()
                    s_v.inc(v.engine_nop())

        NX = (tc == 0)
        with (
            nc.psum_tensor(f"psD0_{tc}", [128, 512], F32) as psD0,
            nc.psum_tensor(f"psD1_{tc}", [128, 512], F32) as psD1,
            nc.psum_tensor(f"psBp0_{tc}", [128, 512], F32) as psBp0,
            nc.psum_tensor(f"psBp1_{tc}", [128, 512], F32) as psBp1,
            nc.psum_tensor(f"psFp3_{tc}", [1, 512], F32) as psFp3,
            nc.psum_tensor(f"psFp4_{tc}", [1, 512], F32) as psFp4,
            nc.Block() as b,
        ):
            psDp = [psD0, psD1]
            psBm3d, psBr3d = psBp0, psBp1
            t0, v0, g0, sc0 = s_t.n, s_v.n, s_g.n, s_s.n
            swp = [s_wa, s_wb]
            w0 = [s_wa.n, s_wb.n]
            bcm3, bcr3 = AFX[:, 1024:1536], AFX[:, 1536:2048]
            wdap = [lambda k: ab(A_WD + k * 128, 128),
                    lambda k: rows_bf[:, k * 128:(k + 1) * 128]]
            wdfl = [ab(A_WD, 3072), rows_bf[:, 0:3072]]
            if NX:
                tmap = [1, 2, 3, 4, 5, 6, 7, 8, 9, 10, 11, 12, 13, 14, 15, 16]
                tmap = [x + 2 for x in tmap]
                vmap = list(range(5, 21))
            else:
                tmap = [1, 2, 3, 4, 6, 7, 9, 10, 12, 13, 15, 16, 18, 20, 22,
                        24]
                vmap = [1, 2, 3, 4, 5, 6, 7, 8, 10, 11, 12, 13, 14, 16, 19,
                        22]
                T_F4A, T_F4B, T_F3H0 = 5, 8, 11
                T_F4C, T_F4D, T_F3H1 = 14, 17, 19
                T_BC = [21, 23]
                V_PRE = [9, 15]
                V_BCC = [17, 20]
                V_X3 = [18, 21]
                S_SQ = [4, 8, 13, 17]
                S_FIN = [9, 18]

            def x2l_next(eng, half, dts, tmpa):
                for dt in dts:
                    xs = xq[:, dt * T + 1024 + half * 512:
                            dt * T + 1024 + half * 512 + 512]
                    eng.tensor_tensor(tmpa, xs, bcm3, ALU.subtract)
                    eng.drain()
                    eng.tensor_tensor(
                        ab(A_X2L + dt * 1024 + half * 512, 512),
                        tmpa, bcr3, ALU.mult)
                    eng.drain()

            def x3m_tiles(eng, half, dts, tmpa):
                for dt in dts:
                    eng.tensor_tensor(
                        tmpa, ab(A_DN + dt * 1024 + half * 512, 512),
                        bcm3, ALU.subtract)
                    eng.drain()
                    eng.tensor_tensor(tmpa, tmpa, bcr3, ALU.mult)
                    eng.drain()
                    xs = xq[:, dt * T + half * 512:dt * T + half * 512 + 512]
                    eng.tensor_tensor(xs, xs, tmpa, ALU.add)
                    eng.drain()

            @b.sync
            def _(sy):
                for mt in range(8):
                    par = mt % 2
                    if mt >= 2:
                        sy.wait_ge(s_t.h, t0 + tmap[2 * mt - 3])
                    swp[par].inc(sy.dma_start(wdfl[par],
                                              wdb_d.ap()[mt]), 16)
                if not NX:
                    sy.wait_ge(s_v.h, v0 + V_X3[1])
                    sy.wait_ge(s_g.h, g0 + 2)
                    s_y.inc(sy.dma_start(
                        y_d.ap()[0:512, 0:1024].rearrange(
                            "(dt p) t -> p dt t", p=128),
                        xq[:, :].rearrange("p (dt t) -> p dt t", t=T)[
                            :, 0:4, 0:1024]), 16)

            @b.tensor
            def _(t):
                def dn_group(i):
                    mt, half = i // 2, i % 2
                    par = mt % 2
                    if half == 0:
                        t.wait_ge(swp[par].h, w0[par] + 16 * (mt // 2 + 1))
                    if i >= 2:
                        t.wait_ge(s_v.h, v0 + vmap[i - 2])
                    for kt in range(24):
                        mm = t.matmul(
                            psDp[i % 2][:, :],
                            wdap[par](kt),
                            ab(A_HIDN + kt * 1024 + half * 512, 512),
                            start=(kt == 0), stop=(kt == 23))
                    s_t.inc(mm)

                if NX:
                    for half in range(2):
                        if half == 1:
                            t.wait_ge(s_v.h, v0 + 1)
                        sl = slice(2048 + half * 512, 2048 + half * 512 + 512)
                        t.matmul(psBm3d[:, :], onesrow_f[0:1, :],
                                 rows[0:1, sl], start=True, stop=True)
                        mm = t.matmul(
                            psBr3d[:, :], onesrow_f[0:1, :],
                            rows[0:1, 1024 + sl.start:1024 + sl.stop],
                            start=True, stop=True)
                        s_t.inc(mm)
                    for i in range(16):
                        dn_group(i)
                else:
                    def f4_batch(lo, first):
                        for k in range(4):
                            mm = t.matmul(psFp4[:, :], onec_bf[:, :],
                                          ab(A_SIL + k * 512, 512),
                                          start=(first and k == 0), stop=False,
                                          skip_group_check=True)
                        s_t.inc(mm)

                    def f3_half(half):
                        for dt in range(8):
                            mm = t.matmul(
                                psFp3[:, :], onec_bf[:, :],
                                ab(A_DN + dt * 1024 + half * 512, 512),
                                start=(dt == 0), stop=(dt == 7),
                                skip_group_check=True)
                        s_t.inc(mm)

                    def bc4(half):
                        t.wait_ge(s_s.h, sc0 + S_FIN[half])
                        if half == 1:
                            t.wait_ge(s_v.h, v0 + V_BCC[0])
                        sl = slice(half * 512, half * 512 + 512)
                        t.matmul(psBm3d[:, :], onesrow_f[0:1, :],
                                 rows[0:1, sl], start=True, stop=True)
                        mm = t.matmul(
                            psBr3d[:, :], onesrow_f[0:1, :],
                            rows[0:1, 1024 + sl.start:1024 + sl.stop],
                            start=True, stop=True)
                        s_t.inc(mm)

                    for i in range(4):
                        dn_group(i)
                    t.wait_ge(s_s.h, sc0 + S_SQ[0])
                    f4_batch(0, True)
                    dn_group(4)
                    dn_group(5)
                    t.wait_ge(s_s.h, sc0 + S_SQ[1])
                    f4_batch(4, False)
                    dn_group(6)
                    dn_group(7)
                    f3_half(0)
                    dn_group(8)
                    dn_group(9)
                    t.wait_ge(s_s.h, sc0 + S_SQ[2])
                    t.wait_ge(s_v.h, v0 + V_PRE[0])
                    f4_batch(0, True)
                    dn_group(10)
                    dn_group(11)
                    t.wait_ge(s_s.h, sc0 + S_SQ[3])
                    f4_batch(4, False)
                    dn_group(12)
                    f3_half(1)
                    dn_group(13)
                    bc4(0)
                    dn_group(14)
                    bc4(1)
                    dn_group(15)

            @b.scalar
            def _(s):
                if not NX:
                    def sq_batch(half, lo):
                        for k in range(4):
                            dt = lo + k
                            s_s.inc(s.activation(
                                ab(A_SIL + k * 512, 512),
                                ab(A_DN + dt * 1024 + half * 512, 512),
                                AF.Square))

                    def fin(half):
                        s.wait_ge(s_v.h, v0 + V_PRE[half])
                        sl = slice(1024 + half * 512, 1024 + half * 512 + 512)
                        s.activation(rows[0:1, sl], rows[0:1, sl], AF.Ln,
                                     bias=EPS, scale=CFF * CFF)
                        s.drain()
                        s.activation(rows[0:1, sl], rows[0:1, sl], AF.Exp,
                                     scale=-0.5)
                        s.drain()
                        s.mul(rows[0:1, sl], rows[0:1, sl], CFF)
                        s.drain()
                        s_s.inc(s.nop())

                    sq_batch(0, 0)
                    s.wait_ge(s_t.h, t0 + T_F4A)
                    sq_batch(0, 4)
                    fin(0)
                    s.wait_ge(s_t.h, t0 + T_F4B)
                    sq_batch(1, 0)
                    s.wait_ge(s_t.h, t0 + T_F4C)
                    sq_batch(1, 4)
                    fin(1)
                    s.wait_ge(s_v.h, v0 + V_X3[1])
                    s.wait_ge(s_g.h, g0 + 2)
                    s_y.inc(s.dma_start(
                        y_d.ap()[512:1024, 0:1024].rearrange(
                            "(dt p) t -> p dt t", p=128),
                        xq[:, :].rearrange("p (dt t) -> p dt t", t=T)[
                            :, 4:8, 0:1024]), 16)

            @b.vector
            def _(v):
                def dn_evac(i):
                    mt, half = i // 2, i % 2
                    v.wait_ge(s_t.h, t0 + tmap[i])
                    v.tensor_copy(
                        ab(DN_B[tc] + mt * 1024 + half * 512, 512),
                        psDp[i % 2][:, :])
                    v.drain()
                    s_v.inc(v.engine_nop())

                if NX:
                    for half in range(2):
                        v.wait_ge(s_t.h, t0 + half + 1)
                        if half == 1:
                            v.wait_ge(s_g.h, g0 + 1)
                        v.tensor_copy(bcm3, psBm3d[:, :])
                        v.tensor_copy(bcr3, psBr3d[:, :])
                        v.drain()
                        s_v.inc(v.engine_nop())
                        x2l_next(v, half, range(0, 5), AFX[:, 0:512])
                        s_v.inc(v.engine_nop())
                    for i in range(16):
                        dn_evac(i)
                else:
                    def pre(half):
                        v.wait_ge(s_t.h, t0 + (T_F3H0 if half == 0
                                               else T_F3H1))
                        ms = rows[0:1, half * 512:half * 512 + 512]
                        ss = rows[0:1, 1024 + half * 512:
                                  1024 + half * 512 + 512]
                        tmp = constc[0:1, 416:928]
                        v.tensor_copy(ms, psFp3[:, :])
                        v.tensor_copy(ss, psFp4[:, :])
                        v.drain()
                        v.tensor_scalar_mul(ms, ms, 1.0 / 1024.0)
                        v.tensor_scalar_mul(ss, ss, 1.0 / 1024.0)
                        v.drain()
                        v.tensor_tensor(tmp, ms, ms, ALU.mult)
                        v.drain()
                        v.tensor_tensor(ss, ss, tmp, ALU.subtract)
                        v.drain()
                        s_v.inc(v.engine_nop())

                    def bcc_x3(half):
                        v.wait_ge(s_t.h, t0 + T_BC[half])
                        if half == 1:
                            v.wait_ge(s_g.h, g0 + 1)
                        v.tensor_copy(bcm3, psBm3d[:, :])
                        v.tensor_copy(bcr3, psBr3d[:, :])
                        v.drain()
                        s_v.inc(v.engine_nop())
                        x3m_tiles(v, half, range(0, 5), AFX[:, 0:512])
                        s_v.inc(v.engine_nop())

                    for i in range(8):
                        dn_evac(i)
                    pre(0)
                    dn_evac(8)
                    dn_evac(9)
                    dn_evac(10)
                    dn_evac(11)
                    dn_evac(12)
                    pre(1)
                    dn_evac(13)
                    bcc_x3(0)
                    dn_evac(14)
                    bcc_x3(1)
                    dn_evac(15)

            @b.gpsimd
            def _(g):
                if NX:
                    for half in range(2):
                        g.wait_ge(s_v.h, v0 + half * 2 + 1)
                        x2l_next(g, half, range(5, 8), AFX[:, 512:1024])
                        s_g.inc(g.engine_nop())
                else:
                    for half in range(2):
                        g.wait_ge(s_v.h, v0 + V_BCC[half])
                        x3m_tiles(g, half, range(5, 8), AFX[:, 512:1024])
                        s_g.inc(g.engine_nop())

        if tc == 0:
            continue
        with (
            nc.psum_tensor(f"psF3_{tc}", [1, 512], F32) as psF3,
            nc.psum_tensor(f"psF4_{tc}", [1, 512], F32) as psF4,
            nc.Block() as b,
        ):
            t0, v0, sc0 = s_t.n, s_v.n, s_s.n

            @b.scalar
            def _(s):
                for half in range(2):
                    for dt in range(8):
                        i = half * 8 + dt
                        if i >= 2:
                            pdt, phalf = (dt - 2) % 8, half - (1 if dt < 2 else 0)
                            s.wait_ge(s_t.h, t0 + phalf * 9 + pdt + 1)
                        s_s.inc(s.activation(
                            ab(A_SIL + (i % 2) * 512, 512),
                            ab(DN_B[tc] + dt * 1024 + half * 512, 512),
                            AF.Square))

            @b.tensor
            def _(t):
                for half in range(2):
                    if half >= 1:
                        t.wait_ge(s_v.h, v0 + half)
                    for dt in range(8):
                        i = half * 8 + dt
                        t.wait_ge(s_s.h, sc0 + i + 1)
                        t.matmul(psF4[:, :], onec_bf[:, :],
                                 ab(A_SIL + (i % 2) * 512, 512),
                                 start=(dt == 0), stop=(dt == 7),
                                 skip_group_check=True)
                        t.drain()
                        s_t.inc(t.nop())
                    for dt in range(8):
                        t.matmul(psF3[:, :], onec_bf[:, :],
                                 ab(DN_B[tc] + dt * 1024 + half * 512, 512),
                                 start=(dt == 0), stop=(dt == 7),
                                 skip_group_check=True)
                    t.drain()
                    s_t.inc(t.nop())

            @b.vector
            def _(v):
                for half in range(2):
                    v.wait_ge(s_t.h, t0 + half * 9 + 9)
                    sl = slice(half * 512, half * 512 + 512)
                    v.tensor_copy(rows[0:1, sl], psF3[:, :])
                    v.tensor_copy(rows[0:1, 1024 + sl.start:1024 + sl.stop],
                                  psF4[:, :])
                    v.drain()
                    s_v.inc(v.engine_nop())

        with nc.Block() as b:
            ln_rows(b, 0, 0, 0, 1024, 1024, ln_scale=CFF)

        with (
            nc.psum_tensor(f"psBm4_{tc}", [128, 512], F32) as psBm4,
            nc.psum_tensor(f"psBr4_{tc}", [128, 512], F32) as psBr4,
            nc.Block() as b,
        ):
            t0, v0, g0 = s_t.n, s_v.n, s_g.n
            bcm4, bcr4 = AFX[:, 1024:1536], AFX[:, 1536:2048]

            @b.tensor
            def _(t):
                for half in range(2):
                    if half >= 1:
                        t.wait_ge(s_v.h, v0 + 1)
                    sl = slice(half * 512, half * 512 + 512)
                    t.matmul(psBm4[:, :], onesrow_f[0:1, :],
                             rows[0:1, sl], start=True, stop=True)
                    t.matmul(psBr4[:, :], onesrow_f[0:1, :],
                             rows[0:1, 1024 + sl.start:1024 + sl.stop],
                             start=True, stop=True)
                    t.drain()
                    s_t.inc(t.nop())

            def x3_tiles(eng, half, dts, tmpa):
                for dt in dts:
                    eng.tensor_tensor(
                        tmpa, ab(DN_B[tc] + dt * 1024 + half * 512, 512),
                        bcm4, ALU.subtract)
                    eng.drain()
                    eng.tensor_tensor(tmpa, tmpa, bcr4, ALU.mult)
                    eng.drain()
                    xs = xq[:, dt * T + TBASE + half * 512:
                            dt * T + TBASE + half * 512 + 512]
                    eng.tensor_tensor(xs, xs, tmpa, ALU.add)
                    eng.drain()

            @b.vector
            def _(v):
                for half in range(2):
                    v.wait_ge(s_t.h, t0 + half + 1)
                    if half >= 1:
                        v.wait_ge(s_g.h, g0 + 4)
                    v.tensor_copy(bcm4, psBm4[:, :])
                    v.tensor_copy(bcr4, psBr4[:, :])
                    v.drain()
                    s_v.inc(v.engine_nop())
                    x3_tiles(v, half, range(0, 4), AFX[:, 0:512])
                    s_v.inc(v.engine_nop(), 4)

            @b.gpsimd
            def _(g):
                for half in range(2):
                    g.wait_ge(s_v.h, v0 + half * 5 + 1)
                    x3_tiles(g, half, range(4, 8), AFX[:, 512:1024])
                    s_g.inc(g.engine_nop(), 4)

        with nc.Block() as b:
            @b.sync
            def _(sy):
                s_y.inc(sy.dma_start(
                    y_d.ap()[0:512, TBASE:TBASE + 1024].rearrange(
                        "(dt p) t -> p dt t", p=128),
                    xq[:, :].rearrange("p (dt t) -> p dt t", t=T)[
                        :, 0:4, TBASE:TBASE + 1024]), 16)

            @b.scalar
            def _(s):
                s_y.inc(s.dma_start(
                    y_d.ap()[512:1024, TBASE:TBASE + 1024].rearrange(
                        "(dt p) t -> p dt t", p=128),
                    xq[:, :].rearrange("p (dt t) -> p dt t", t=T)[
                        :, 4:8, TBASE:TBASE + 1024]), 16)

    with nc.Block() as b:
        @b.sync
        def _(sy):
            sy.wait_ge(s_y.h, s_y.n)

    ctx.close()
    return nc


_NC_CACHE = {}


def _get_nc():
    if "nc" not in _NC_CACHE:
        _NC_CACHE["nc"] = build_nc()
    return _NC_CACHE["nc"]


def _host_tables():
    if "tabs" in _NC_CACHE:
        return _NC_CACHE["tabs"]
    inv_freq = 1.0 / ROPE_BASE ** (np.arange(0, 64, 2, dtype=np.float32) / 64)
    pos = np.arange(S, dtype=np.float32)
    fr = pos[:, None] * inv_freq[None, :]
    cos_t = np.cos(fr).astype(np.float32)
    sin_t = np.sin(fr).astype(np.float32)
    cos_fm = np.ascontiguousarray(np.tile(cos_t.T, (4, 1)))
    sin_fm = np.ascontiguousarray(np.tile(sin_t.T, (4, 1)))
    cos_tm = np.ascontiguousarray(np.tile(cos_t, (1, 32))).astype(BF)
    sgn = np.where((np.arange(D) // 128) % 2 == 0, -1.0, 1.0).astype(np.float32)
    sin_tm = np.ascontiguousarray(np.tile(sin_t, (1, 32)) * sgn[None, :]) \
        .astype(BF)
    _NC_CACHE["tabs"] = (cos_fm, sin_fm, cos_tm, sin_tm)
    return _NC_CACHE["tabs"]


def kernel(L0_attn_x, L0_attn_w_q, L0_attn_w_kv_down, L0_attn_w_kv_up,
           L0_attn_w_out, L0_attn_w_scorer, L0_ffn_w_gate, L0_ffn_w_up,
           L0_ffn_w_down):
    x = np.ascontiguousarray(np.asarray(L0_attn_x, dtype=np.float32))
    w_q = np.asarray(L0_attn_w_q, dtype=np.float32)
    w_kvd = np.asarray(L0_attn_w_kv_down, dtype=np.float32)
    w_kvu = np.asarray(L0_attn_w_kv_up, dtype=np.float32)
    w_out = np.asarray(L0_attn_w_out, dtype=np.float32)
    w_sc = np.asarray(L0_attn_w_scorer, dtype=np.float32)
    w_g = np.asarray(L0_ffn_w_gate, dtype=np.int32)
    w_u = np.asarray(L0_ffn_w_up, dtype=np.int32)
    w_d = np.asarray(L0_ffn_w_down, dtype=np.int32)

    cos_fm, sin_fm, cos_tm, sin_tm = _host_tables()

    wqT = np.ascontiguousarray(w_q.T[:, PERM])
    wkvdT = np.ascontiguousarray(w_kvd.T)
    wkuT_full = w_kvu.T
    wkuT = np.ascontiguousarray(
        np.concatenate([wkuT_full[:, 0:D][:, PERM], wkuT_full[:, D:2 * D]],
                       axis=1))
    woutT = np.ascontiguousarray(w_out.T)
    wscT = np.ascontiguousarray(w_sc.reshape(8, 128).T)
    wgT = np.ascontiguousarray(w_g.T)
    wuT = np.ascontiguousarray(w_u.T)
    wdT = np.ascontiguousarray(w_d.T)

    shared = {
        "wqT": wqT, "wkvdT": wkvdT, "wkuT": wkuT, "woutT": woutT,
        "wscT": wscT, "wgT": wgT, "wuT": wuT, "wdT": wdT,
        "costm": cos_tm, "sintm": sin_tm,
    }
    in_maps = []
    for core in range(8):
        bidx, hf = core // 2, core % 2
        xTb = np.ascontiguousarray(x[bidx].T)
        qoff = hf * T
        m = dict(shared)
        m["xT"] = xTb
        m["xTq"] = np.ascontiguousarray(xTb[:, qoff:qoff + T])
        m["xtm"] = x[bidx]
        m["qiota"] = np.arange(qoff, qoff + T, dtype=np.float32).reshape(1, T)
        m["cosfm"] = np.ascontiguousarray(cos_fm[:, qoff:qoff + T]).astype(BF)
        m["sinpfm"] = np.ascontiguousarray(sin_fm[:, qoff:qoff + T]).astype(BF)
        m["sinnfm"] = np.ascontiguousarray(-sin_fm[:, qoff:qoff + T]).astype(BF)
        in_maps.append(m)

    nc = _get_nc()
    res = run_bass_kernel_spmd(nc, in_maps, core_ids=list(range(8)))

    out = np.empty((4, S, D), dtype=np.float32)
    for core in range(8):
        bidx, hf = core // 2, core % 2
        out[bidx, hf * T:(hf + 1) * T, :] = res.results[core]["y"].T
    return out
